# version 07
# baseline (speedup 1.0000x reference)
import contextlib

import numpy as np
import ml_dtypes

import concourse.bass as bass
import concourse.mybir as mybir
from concourse.bass_utils import run_bass_kernel_spmd

F32 = mybir.dt.float32
BF16 = mybir.dt.bfloat16
AF = mybir.ActivationFunctionType
OP = mybir.AluOpType
AX = mybir.AxisListType.X

NCORES = 8
NSEQ = 2
S = 2048
DM = 1024
NT = 16
NH = 16
HD = 64
QB = 256
NQB = S // QB
NE = 8
BATTN = dict(nqb=8, lsum=True, bcast=True, ctx=True, mask=True, epi=True)
EPS = 1e-5
SCA = 1.0 / 32.0
SCB = 1.0 / 8.0

ENGINES = ("tensor", "vector", "scalar", "gpsimd", "sync")
DMA_QUEUES = ("sync", "gpsimd", "scalar")
DMA_POOL = 8


class Sched:

    def __init__(self):
        self.ops = []
        self.last_w = {}
        self.readers = {}
        self.ecount = {}

    def add(self, eng, fn, rd=(), wr=(), dma=False):
        oid = len(self.ops)
        deps = set()
        for k in rd:
            w = self.last_w.get(k)
            if w is not None:
                deps.add(w)
        for k in wr:
            w = self.last_w.get(k)
            if w is not None:
                deps.add(w)
            for r in self.readers.get(k, ()):
                deps.add(r)
        for k in rd:
            self.readers.setdefault(k, []).append(oid)
        for k in wr:
            self.last_w[k] = oid
            self.readers[k] = []
        eseq = self.ecount.get(eng, 0)
        self.ecount[eng] = eseq + 1
        self.ops.append(dict(eng=eng, fn=fn, dma=dma, deps=deps, inc=False,
                             eseq=eseq))
        return oid

    def fence(self, rd, wr):
        self.add("vector", lambda e: e.engine_nop(), rd=tuple(rd), wr=tuple(wr))

    def finalize(self, nc, es):
        for o in self.ops:
            kept = set()
            for d in o["deps"]:
                p = self.ops[d]
                if (p["eng"] == o["eng"] and not p["dma"] and not o["dma"]):
                    if p["eng"] == "tensor":
                        continue
                    if o["eseq"] - p["eseq"] > 2:
                        continue
                kept.add(d)
            o["deps"] = kept
        for o in self.ops:
            for d in o["deps"]:
                self.ops[d]["inc"] = True
        for o in self.ops:
            if o["dma"]:
                o["inc"] = True
        csem = {e: es.enter_context(nc.semaphore(f"c_{e}")) for e in ENGINES}
        dsem = {q: [es.enter_context(nc.semaphore(f"d_{q}_{i}"))
                    for i in range(DMA_POOL)] for q in DMA_QUEUES}
        ccount = {e: 0 for e in ENGINES}
        dcount = {q: 0 for q in DMA_QUEUES}
        slot_uses = {q: [0] * DMA_POOL for q in DMA_QUEUES}
        for o in self.ops:
            if o["dma"]:
                q = o["eng"]
                slot = dcount[q] % DMA_POOL
                dcount[q] += 1
                o["sem"] = dsem[q][slot]
                o["slot_prev"] = slot_uses[q][slot]
                slot_uses[q][slot] += 16
                o["val"] = slot_uses[q][slot]
            elif o["inc"]:
                e = o["eng"]
                ccount[e] += 1
                o["sem"] = csem[e]
                o["val"] = ccount[e]

        def emitter(engname):
            def run(eng):
                water = {}
                for o in self.ops:
                    if o["eng"] != engname:
                        continue
                    waits = {}
                    for d in o["deps"]:
                        p = self.ops[d]
                        if waits.get(p["sem"], -1) < p["val"]:
                            waits[p["sem"]] = p["val"]
                    if o["dma"] and o["slot_prev"] > 0:
                        if waits.get(o["sem"], -1) < o["slot_prev"]:
                            waits[o["sem"]] = o["slot_prev"]
                    for sem, v in waits.items():
                        if water.get(sem.num, -1) >= v:
                            continue
                        eng.wait_ge(sem, v)
                        water[sem.num] = v
                    if o["fn"] is None:
                        continue
                    inst = o["fn"](eng)
                    if o["dma"]:
                        inst.then_inc(o["sem"], 16)
                    elif o["inc"]:
                        inst.then_inc(o["sem"], 1)
            return run

        with nc.Block() as block:
            block.tensor(emitter("tensor"))
            block.vector(emitter("vector"))
            block.scalar(emitter("scalar"))
            block.gpsimd(emitter("gpsimd"))
            block.sync(emitter("sync"))
        return len(self.ops)


def build_nc(upto="full", dbg=False):
    nc = bass.Bass("TRN2", target_bir_lowering=False)
    sch = Sched()
    es = contextlib.ExitStack()

    def din(name, shape, dt=F32):
        return nc.dram_tensor(name, shape, dt, kind="ExternalInput")

    d_x = din("x", [NSEQ, S, DM])
    d_wqkv = din("wqkv", [6, 8, 128, 512])
    d_bqkv_fm = din("bqkv_fm", [128, 24])
    d_bqkv_row = din("bqkv_row", [1, 3 * DM])
    d_wout = din("wout", [8, 128, DM])
    d_bout_rep = din("bout_rep", [128, DM])
    d_winproj = din("winproj", [3, 2, 8, 128, 512])
    d_binproj_fm = din("binproj_fm", [128, 24])
    d_binproj_row = din("binproj_row", [1, 3 * DM])
    d_woutproj = din("woutproj", [8, 128, DM])
    d_boutproj_rep = din("boutproj_rep", [128, DM])
    d_wgate = din("wgate", [128, 64])
    d_bgate = din("bgate", [8, 1])
    d_wexp = din("wexp", [NE, 128, 8 * DM])
    d_bexp = din("bexp", [8, DM])
    d_ln1w = din("ln1w_rep", [128, DM])
    d_ln1b = din("ln1b_rep", [128, DM])
    d_ln2w = din("ln2w_rep", [128, DM])
    d_ln2b = din("ln2b_rep", [128, DM])
    d_identb = din("identb", [128, 128], BF16)
    d_identf = din("identf", [128, 128])
    d_onescol = din("onescol", [128, 1], BF16)
    d_rowconst = din("rowconst", [1, 384], BF16)
    d_trimask = din("trimask", [128, 128], BF16)
    d_ce = din("ce", [S, DM // 2])
    d_se = din("se", [S, DM // 2])

    d_out = nc.dram_tensor("out", [NSEQ, S, DM], F32, kind="ExternalOutput")
    d_dbg = nc.dram_tensor("dbg", [8, 128, 1024], BF16, kind="ExternalOutput") \
        if dbg else None
    d_dbgf = nc.dram_tensor("dbgf", [4, 128, 1024], F32, kind="ExternalOutput") \
        if dbg else None
    d_u = nc.dram_tensor("u_scr", [NSEQ, S, DM], F32)
    d_y1 = nc.dram_tensor("y1_scr", [S, DM], BF16)
    d_mb = nc.dram_tensor("m_scr", [128], F32)

    sb = lambda n, shape, dt: es.enter_context(nc.sbuf_tensor(n, shape, dt))
    psb = lambda n, shape, dt=F32: es.enter_context(nc.psum_tensor(n, shape, dt))

    arX = sb("arX", [128, 16384], BF16)
    arQ = sb("arQ", [128, 16384], BF16)
    arK = sb("arK", [128, 16384], BF16)
    arV = sb("arV", [128, 16384], BF16)
    w1 = sb("w1", [128, 8192], BF16)
    xs = sb("xs", [128, 2048], F32)
    fsb = sb("fsb", [128, 2048], F32)
    tma = sb("tma", [128, 2048], BF16)
    scr = sb("scr", [128, 896], F32)
    sc2 = sb("sc2", [128, 144], F32)
    tbl = sb("tbl", [128, 2048], F32)
    reps = sb("reps", [128, 6 * DM], BF16)
    vrow = sb("vrow", [128, 3072], BF16)
    y1s = sb("y1s", [128, 2048], BF16)
    identb = sb("identb_s", [128, 128], BF16)
    identf = sb("identf_s", [128, 128], F32)
    onescol = sb("onescol_s", [128, 1], BF16)
    trimask = sb("trimask_s", [128, 128], BF16)
    gts = sb("gts", [128, 256], F32)
    rbb = sb("rbb", [128, 256], F32)
    mrow = sb("mrow", [128, 128], BF16)
    wg8 = sb("wg8", [128, 8], BF16)
    etmp = sb("etmp", [128, 1024], BF16)
    geb = sb("geb", [8, DM], BF16)
    gwt = sb("gwt", [128, 64], BF16)
    gbb = sb("gbb", [8, 1], F32)
    bqfm = sb("bqfm", [128, 24], F32)
    epsc = sb("epsc", [128, 1], F32)
    dsink = sb("dsink", [128, 1024], BF16)
    q2b = dsink[:, 0:512]
    bifm = sb("bifm", [128, 24], F32)


    psS = [psb("psS0", [128, 1024]), psb("psS1", [128, 1024])]
    psC = [psb("psC0", [128, 512]), psb("psC1", [128, 512])]
    psL = psb("psL", [128, 512])
    ptr = psb("ptr", [128, 1024], BF16)

    ones_row = vrow[0:1, 2048:2176]
    maskA_row = vrow[0:1, 2176:2304]
    maskB_row = vrow[0:1, 2304:2432]

    def T(fn, rd=(), wr=()):
        sch.add("tensor", fn, rd, wr)

    def V(fn, rd=(), wr=()):
        sch.add("vector", fn, rd, wr)

    def A_(fn, rd=(), wr=()):
        sch.add("scalar", fn, rd, wr)

    def G(fn, rd=(), wr=()):
        sch.add("gpsimd", fn, rd, wr)

    def DMA(queue, fn, rd=(), wr=()):
        sch.add(queue, fn, rd, wr, dma=True)

    def mm(out, lhsT, rhs, start, stop, tp=None):
        return lambda e, o=out, l=lhsT, r=rhs, s=start, st=stop, t=tp: \
            e.matmul(o, l, r, start=s, stop=st, tile_position=t)

    sch.add("vector", lambda e: e.memset(epsc[:, :], EPS), wr=("epsc",))
    for q, src, dst, key in (
        ("sync", d_bqkv_fm[:, :], bqfm[:, :], "bqfm"),
        ("sync", d_binproj_fm[:, :], bifm[:, :], "bifm"),
        ("sync", d_identf[:, :], identf[:, :], "identf"),
        ("sync", d_bgate[:, :], gbb[:, :], "gbb"),
        ("sync", d_identb[:, :], identb[:, :], "identb"),
        ("sync", d_onescol[:, :], onescol[:, :], "onescol"),
        ("sync", d_trimask[:, :], trimask[:, :], "trimask"),
        ("sync", d_rowconst[:, :], vrow[0:1, 2048:2432], "vrowc"),
        ("gpsimd", d_bqkv_row[0:1, 2048:3072], vrow[0:1, 0:1024], "bvA"),
        ("gpsimd", d_binproj_row[0:1, 2048:3072], vrow[0:1, 1024:2048], "bvB"),
        ("gpsimd", d_bexp[:, :], geb[:, :], "geb"),
        ("gpsimd", d_wgate[:, :], gwt[:, :], "gwt"),
        ("gpsimd", d_bout_rep[:, :], reps[:, 0:DM], "reps0"),
        ("gpsimd", d_boutproj_rep[:, :], reps[:, DM:2 * DM], "reps1"),
        ("gpsimd", d_ln1w[:, :], reps[:, 2 * DM:3 * DM], "reps2"),
        ("gpsimd", d_ln1b[:, :], reps[:, 3 * DM:4 * DM], "reps3"),
        ("gpsimd", d_ln2w[:, :], reps[:, 4 * DM:5 * DM], "reps4"),
        ("gpsimd", d_ln2b[:, :], reps[:, 5 * DM:6 * DM], "reps5"),
    ):
        DMA(q, (lambda e, s=src, d=dst: e.dma_start(out=d, in_=s)), rd=(), wr=(key,))

    def ln_stats(src_ap, keys, s):
        b = 24 * s
        A_(lambda e, sp=src_ap: e.activation(dsink[:, :], sp, AF.Identity,
                                             accum_out=sc2[:, b + 21:b + 22]),
           rd=tuple(keys) + ("dsink",), wr=(f"sum{s}", "dsink"))
        A_(lambda e, sp=src_ap: e.activation(dsink[:, :], sp, AF.Square,
                                             accum_out=sc2[:, b + 22:b + 23]),
           rd=tuple(keys) + ("dsink",), wr=(f"sq{s}", "dsink"))
        V(lambda e: e.tensor_scalar(sc2[:, b + 16:b + 17], sc2[:, b + 21:b + 22],
                                    1.0 / 1024.0, None, OP.mult),
          rd=(f"sum{s}",), wr=(f"mean{s}",))
        V(lambda e: e.tensor_mul(sc2[:, b + 18:b + 19], sc2[:, b + 16:b + 17],
                                 sc2[:, b + 16:b + 17]),
          rd=(f"mean{s}",), wr=(f"tmp{s}",))
        V(lambda e: e.scalar_tensor_tensor(sc2[:, b + 17:b + 18],
                                           sc2[:, b + 22:b + 23], 1.0 / 1024.0,
                                           sc2[:, b + 18:b + 19],
                                           OP.mult, OP.subtract),
          rd=(f"sq{s}", f"tmp{s}"), wr=(f"var{s}",))
        A_(lambda e: e.activation(sc2[:, b + 19:b + 20], sc2[:, b + 17:b + 18],
                                  AF.Ln, bias=epsc[:, 0:1]),
           rd=(f"var{s}", "epsc"), wr=(f"rstd{s}",))
        A_(lambda e: e.activation(sc2[:, b + 19:b + 20], sc2[:, b + 19:b + 20],
                                  AF.Exp, scale=-0.5),
           rd=(f"rstd{s}",), wr=(f"rstd{s}",))
        V(lambda e: e.tensor_scalar(sc2[:, b + 20:b + 21], sc2[:, b + 16:b + 17],
                                    sc2[:, b + 19:b + 20], -1.0, OP.mult, OP.mult),
          rd=(f"mean{s}", f"rstd{s}"), wr=(f"nm{s}",))

    def ln_stats_v(src_ap, keys, s):
        b = 24 * s
        V(lambda e, sp=src_ap: e.bn_stats(sc2[:, b:b + 6], sp[:, 0:512]),
          rd=tuple(keys), wr=(f"bn{s}a",))
        V(lambda e, sp=src_ap: e.bn_stats(sc2[:, b + 6:b + 12], sp[:, 512:1024]),
          rd=tuple(keys), wr=(f"bn{s}b",))
        V(lambda e: e.bn_aggr(sc2[:, b + 12:b + 14], sc2[:, b:b + 6]),
          rd=(f"bn{s}a",), wr=(f"mv{s}a",))
        V(lambda e: e.bn_aggr(sc2[:, b + 14:b + 16], sc2[:, b + 6:b + 12]),
          rd=(f"bn{s}b",), wr=(f"mv{s}b",))
        V(lambda e: e.tensor_add(sc2[:, b + 16:b + 17], sc2[:, b + 12:b + 13],
                                 sc2[:, b + 14:b + 15]),
          rd=(f"mv{s}a", f"mv{s}b"), wr=(f"mean{s}",))
        V(lambda e: e.tensor_scalar(sc2[:, b + 16:b + 17], sc2[:, b + 16:b + 17],
                                    0.5, None, OP.mult),
          rd=(f"mean{s}",), wr=(f"mean{s}",))
        V(lambda e: e.tensor_sub(sc2[:, b + 18:b + 19], sc2[:, b + 12:b + 13],
                                 sc2[:, b + 14:b + 15]),
          rd=(f"mv{s}a", f"mv{s}b"), wr=(f"tmp{s}",))
        V(lambda e: e.tensor_mul(sc2[:, b + 18:b + 19], sc2[:, b + 18:b + 19],
                                 sc2[:, b + 18:b + 19]),
          rd=(f"tmp{s}",), wr=(f"tmp{s}",))
        V(lambda e: e.tensor_add(sc2[:, b + 17:b + 18], sc2[:, b + 13:b + 14],
                                 sc2[:, b + 15:b + 16]),
          rd=(f"mv{s}a", f"mv{s}b"), wr=(f"var{s}",))
        V(lambda e: e.tensor_scalar(sc2[:, b + 17:b + 18], sc2[:, b + 17:b + 18],
                                    0.5, None, OP.mult),
          rd=(f"var{s}",), wr=(f"var{s}",))
        V(lambda e: e.scalar_tensor_tensor(sc2[:, b + 17:b + 18],
                                           sc2[:, b + 18:b + 19], 0.25,
                                           sc2[:, b + 17:b + 18], OP.mult, OP.add),
          rd=(f"tmp{s}", f"var{s}"), wr=(f"var{s}",))
        A_(lambda e: e.activation(sc2[:, b + 19:b + 20], sc2[:, b + 17:b + 18],
                                  AF.Ln, bias=epsc[:, 0:1]),
           rd=(f"var{s}", "epsc"), wr=(f"rstd{s}",))
        A_(lambda e: e.activation(sc2[:, b + 19:b + 20], sc2[:, b + 19:b + 20],
                                  AF.Exp, scale=-0.5),
           rd=(f"rstd{s}",), wr=(f"rstd{s}",))
        V(lambda e: e.tensor_scalar(sc2[:, b + 20:b + 21], sc2[:, b + 16:b + 17],
                                    sc2[:, b + 19:b + 20], -1.0, OP.mult, OP.mult),
          rd=(f"mean{s}", f"rstd{s}"), wr=(f"nm{s}",))

    def ln_apply(dst_ap, src_ap, rdk, wrk, s, eng="scalar"):
        b = 24 * s
        if eng == "scalar":
            A_(lambda e, d=dst_ap, sp=src_ap: e.activation(
                d, sp, AF.Identity, bias=sc2[:, b + 20:b + 21],
                scale=sc2[:, b + 19:b + 20]),
               rd=tuple(rdk) + (f"rstd{s}", f"nm{s}"), wr=tuple(wrk))
        else:
            V(lambda e, d=dst_ap, sp=src_ap: e.tensor_scalar(
                d, sp, sc2[:, b + 19:b + 20], sc2[:, b + 20:b + 21],
                OP.mult, OP.add),
              rd=tuple(rdk) + (f"rstd{s}", f"nm{s}"), wr=tuple(wrk))

    def tm_to_fm(tmslot_ap, fm_h, t, rdk, prefix):
        for dc in range(8):
            T(lambda e, sp=tmslot_ap, d=dc: e.transpose(
                ptr[:, d * 128:(d + 1) * 128], sp[:, d * 128:(d + 1) * 128],
                identb[:, :]),
              rd=tuple(rdk) + ("identb",), wr=(f"ptr{dc}",))
        out_ap = fm_h[:, :].rearrange("p (c q) -> p c q", c=8)[:, :, t * 128:(t + 1) * 128]
        in_ap = ptr[:, :].rearrange("p (c q) -> p c q", c=8)
        V(lambda e, o=out_ap, i=in_ap: e.tensor_copy(o, i),
          rd=tuple(f"ptr{dc}" for dc in range(8)),
          wr=tuple(f"{prefix}_{dc}_{t}" for dc in range(8)))

    def qk_keys(pref):
        return tuple(f"{pref}_{oc}_tc{tc}" for oc in range(8) for tc in range(4))

    def fm_keys(prefix, dc=None, ts=None):
        dcs = range(8) if dc is None else [dc]
        tts = range(NT) if ts is None else ts
        return tuple(f"{prefix}_{d}_{t}" for d in dcs for t in tts)


    def dump(idx, ap, keys, f32=False, w=1024):
        dst = (d_dbgf if f32 else d_dbg)
        DMA("sync", lambda e, i=idx, a=ap, ww=w: e.dma_start(
            out=dst[i, :, 0:ww], in_=a), rd=tuple(keys),
            wr=(f"outd_dbg{'f' if f32 else ''}{idx}",))

    def emit_seq(si, tail_cb=None):
        sq = f"s{si}"

        for t in range(NT):
            xslot = xs[:, (t % 2) * 1024:(t % 2) * 1024 + 1024]
            xk = f"xs{t % 2}"
            DMA("sync", lambda e, tt=t, d=xslot: e.dma_start(
                out=d, in_=d_x[si, tt * 128:(tt + 1) * 128, :]), rd=(), wr=(xk,))
            st = t % 2
            ln_stats(xslot, (xk,), st)
            tslot = tma[:, st * 1024:st * 1024 + 1024]
            tk = f"tma{st}"
            ln_apply(tslot, xslot, (xk,), (tk,), st)
            tm_to_fm(tslot, arX, t, (tk,), "h0")
            if tail_cb is not None:
                tail_cb[0](t)

        if tail_cb is not None:
            tail_cb[1]()
        def emit_qkv(src_h, src_pref, wq_dram, bias_sb, dst_list, vrow_off):
            for og in range(4):
                wk = f"w1_{og % 2}"
                for dc in range(8):
                    DMA("gpsimd", lambda e, d=dc, o=og: e.dma_start(
                        out=w1[:, (o % 2) * 4096 + d * 512:
                               (o % 2) * 4096 + (d + 1) * 512],
                        in_=wq_dram(o, d)),
                        rd=(), wr=(wk,))
                dsth, pref = dst_list[og // 2]
                for tc in range(4):
                    for oc4 in range(4):
                        oc = (og % 2) * 4 + oc4
                        un, hh = oc4 // 2, oc4 % 2
                        for dc in range(8):
                            T(mm(psS[un][:, hh * 512:hh * 512 + 512],
                                 w1[:, (og % 2) * 4096 + dc * 512 + oc4 * 128:
                                    (og % 2) * 4096 + dc * 512 + oc4 * 128 + 128],
                                 src_h[:, dc * S + tc * 512: dc * S + tc * 512 + 512],
                                 dc == 0, dc == 7),
                              rd=(wk,) + fm_keys(src_pref, dc, range(tc * 4, tc * 4 + 4)),
                              wr=(f"psS{un}_{hh}",))
                        A_(lambda e, dh=dsth, o=oc, c=tc, u=un, h=hh,
                           b=bias_sb[:, (og // 2) * 8 + oc:(og // 2) * 8 + oc + 1]:
                           e.activation(dh[:, o * S + c * 512: o * S + c * 512 + 512],
                                        psS[u][:, h * 512:h * 512 + 512],
                                        AF.Identity, bias=b),
                           rd=(f"psS{un}_{hh}", "biasfm"),
                           wr=(f"{pref}_{oc}_tc{tc}",))
            for vog in range(2):
                wk = f"w1_{vog}"
                for dc in range(8):
                    DMA("gpsimd", lambda e, d=dc, o=vog: e.dma_start(
                        out=w1[:, o * 4096 + d * 512: o * 4096 + (d + 1) * 512],
                        in_=wq_dram(4 + o, d)),
                        rd=(), wr=(wk,))
                for t in range(NT):
                    pc = psC[t % 2]
                    T(mm(pc[:, :], ones_row,
                         vrow[0:1, vrow_off + vog * 512: vrow_off + vog * 512 + 512],
                         True, False),
                      rd=("vrowc", "bvrow"), wr=(f"psC{t % 2}",))
                    for dc in range(8):
                        T(mm(pc[:, :], src_h[:, dc * S + t * 128: dc * S + t * 128 + 128],
                             w1[:, vog * 4096 + dc * 512: vog * 4096 + (dc + 1) * 512],
                             False, dc == 7),
                          rd=(wk,) + fm_keys(src_pref, dc, [t]), wr=(f"psC{t % 2}",))
                    V(lambda e, tt=t, o=vog, p=pc: e.tensor_copy(
                        arV[:, tt * DM + o * 512: tt * DM + o * 512 + 512], p[:, :]),
                      rd=(f"psC{t % 2}",), wr=(f"v_{t}_{vog}",))

        sch.fence(("bqfm", "bvA"), ("biasfm", "bvrow"))
        emit_qkv(arX, "h0", lambda s_, d: d_wqkv[s_, d], bqfm,
                 [(arQ, "q"), (arK, "k")], 0)

        sch.fence(("w1_0", "w1_1"), tuple(f"w1res{d}" for d in range(8)))
        for dc in range(8):
            DMA("gpsimd", lambda e, d=dc: e.dma_start(
                out=w1[:, d * DM:(d + 1) * DM], in_=d_wout[d]),
                rd=(), wr=(f"w1res{dc}",))

        def emit_norms_A():
            for which, pref, src in (("q", "q", arQ), ("k", "k", arK)):
                for tc in range(4):
                    for dc in range(8):
                        V(lambda e, s_=src, d=dc, c=tc: e.tensor_mul(
                            q2b, s_[:, d * S + c * 512: d * S + c * 512 + 512],
                            s_[:, d * S + c * 512: d * S + c * 512 + 512]),
                          rd=(f"{pref}_{dc}_tc{tc}",), wr=("q2b", "dsink"))
                        T(mm(psL[0:1, 0:512], onescol[:, 0:1], q2b,
                             dc == 0, dc == 7),
                          rd=("q2b", "dsink", "onescol"), wr=("psL",))
                    if which == "q":
                        for h in range(2):
                            V(lambda e, c=tc, hh=h: e.reduce_max(
                                sc2[0:1, 102 + 2 * c + hh: 103 + 2 * c + hh],
                                psL[0:1, hh * 256:hh * 256 + 256], axis=AX),
                              rd=("psL",), wr=(f"qmaxA{2 * tc + h}",))
                    else:
                        V(lambda e, c=tc: e.reduce_max(
                            sc2[0:1, 110 + c:111 + c], psL[0:1, 0:512], axis=AX),
                          rd=("psL",), wr=(f"kmaxAt{tc}",))
            V(lambda e: e.reduce_max(sc2[0:1, 114:115], sc2[0:1, 110:114], axis=AX),
              rd=tuple(f"kmaxAt{c}" for c in range(4)), wr=("kmaxA",))
            V(lambda e: e.tensor_scalar(sc2[0:1, 115:123], sc2[0:1, 102:110],
                                        sc2[0:1, 114:115], SCA * SCA, OP.mult, OP.mult),
              rd=tuple(f"qmaxA{j}" for j in range(8)) + ("kmaxA",), wr=("mtA",))
            A_(lambda e: e.activation(sc2[0:1, 115:123], sc2[0:1, 115:123], AF.Ln),
               rd=("mtA",), wr=("mtA",))
            A_(lambda e: e.activation(sc2[0:1, 115:123], sc2[0:1, 115:123], AF.Exp,
                                      scale=0.5), rd=("mtA",), wr=("mtA",))
            V(lambda e: e.tensor_scalar(mrow[0:1, 0:8], sc2[0:1, 115:123], -1.0,
                                        None, OP.mult), rd=("mtA",), wr=("negmA",))
            for j in range(8):
                T(mm(psL[:, j:j + 1], ones_row, mrow[0:1, j:j + 1], True, True),
                  rd=("negmA", "vrowc"), wr=("psL",))
            V(lambda e: e.tensor_copy(scr[:, 4:12], psL[:, 0:8]),
              rd=("psL",), wr=("biasA",))

        if dbg and si == 0:
            dump(0, arX[:, 0:1024], fm_keys("h0", 0, range(8)))
            dump(1, arQ[:, 0:1024], ("q_0_tc0", "q_0_tc1"))
            dump(2, arK[:, 0:1024], ("k_0_tc0", "k_0_tc1"))
            dump(3, arV[:, 0:1024], ("v_0_0", "v_0_1"))

        emit_norms_A()
        if dbg and si == 0:
            dump(0, scr[:, 0:1024], ("biasA",), f32=True)
            DMA("sync", lambda e: e.dma_start(out=d_dbgf[3, :, 0:96],
                                              in_=sc2[:, 0:96]),
                rd=("biasA", "mtA"), wr=("outd_dbgsc2",))

        sch.fence(fm_keys("h0"), tuple(f"pt{k}" for k in range(NT)) +
                  tuple(f"cx{d}" for d in range(8)))
        for qb in range(NQB):
            for ku in range(4):
                un = ku % 2
                for ki in range(4):
                    kt = ku * 4 + ki
                    for dc in range(8):
                        T(mm(psS[un][:, ki * 256:ki * 256 + 256],
                             arK[:, dc * S + kt * 128: dc * S + kt * 128 + 128],
                             arQ[:, dc * S + qb * 256: dc * S + qb * 256 + 256],
                             dc == 0, dc == 7),
                          rd=(f"k_{dc}_tc{kt // 4}", f"q_{dc}_tc{qb // 2}"),
                          wr=(f"psS{un}_{ki // 2}",))
                A_(lambda e, u=un, k=ku, q=qb: e.activation(
                    arX[:, k * 1024: k * 1024 + 1024], psS[u][:, :], AF.Exp,
                    bias=scr[:, 4 + q:5 + q], scale=SCA),
                   rd=(f"psS{un}_0", f"psS{un}_1", "biasA"),
                   wr=tuple(f"pt{4 * ku + i}" for i in range(4)))
            if dbg and si == 0 and qb == 0:
                dump(4, arX[:, 0:1024], tuple(f"pt{k}" for k in range(4)))
            for kt in range(NT):
                T(mm(psL[0:1, 0:256], onescol[:, 0:1],
                     arX[:, kt * 256: kt * 256 + 256], kt == 0, kt == NT - 1),
                  rd=(f"pt{kt}", "onescol"), wr=("psL",))
            V(lambda e: e.tensor_copy(scr[0:1, 96:352], psL[0:1, 0:256]),
              rd=("psL",), wr=("lrow",))
            for dc in range(8):
                pc = psC[dc % 2]
                for kt in range(NT):
                    T(mm(pc[:, 0:256],
                         arV[:, kt * DM + dc * 128: kt * DM + dc * 128 + 128],
                         arX[:, kt * 256: kt * 256 + 256], kt == 0, kt == NT - 1),
                      rd=(f"v_{kt}_{dc // 4}", f"pt{kt}"), wr=(f"psC{dc % 2}",))
                V(lambda e, d=dc, p=pc: e.tensor_copy(
                    arX[:, 8192 + d * 256: 8192 + (d + 1) * 256], p[:, 0:256]),
                  rd=(f"psC{dc % 2}",), wr=(f"cx{dc}",))
            for j in range(2):
                T(lambda e, jj=j: e.transpose(
                    psL[:, jj:jj + 1], scr[0:1, 96 + jj * 128: 96 + (jj + 1) * 128],
                    identf[0:1, 0:1]),
                  rd=("lrow", "identf"), wr=("psL",))
            V(lambda e, q=qb: e.reciprocal(scr[:, 76 + q * 2: 78 + q * 2],
                                           psL[:, 0:2]),
              rd=("psL",), wr=(f"rtm{qb}",))
            if dbg and si == 0 and qb == 0:
                dump(5, arX[:, 8192:9216], tuple(f"cx{d}" for d in range(4)))
                dump(1, scr[:, 0:1024], (f"rtm{qb}", "lrow"), f32=True)
            for j in range(2):
                t = qb * 2 + j
                for dc in range(8):
                    for vc in range(2):
                        T(mm(psS[0][:, vc * 512:vc * 512 + 512],
                             arX[:, 8192 + dc * 256 + j * 128:
                                 8192 + dc * 256 + j * 128 + 128],
                             w1[:, dc * DM + vc * 512: dc * DM + vc * 512 + 512],
                             dc == 0, dc == 7),
                          rd=(f"cx{dc}", f"w1res{dc}"), wr=(f"psS0_{vc}",))
                aslot = fsb[:, (j % 2) * 1024:(j % 2) * 1024 + 1024]
                ak = f"fs{j % 2}"
                V(lambda e, a=aslot, q=qb, jj=j: e.scalar_tensor_tensor(
                    a, psS[0][:, :], scr[:, 76 + q * 2 + jj: 77 + q * 2 + jj],
                    reps[:, 0:DM], OP.mult, OP.add),
                  rd=("psS0_0", "psS0_1", f"rtm{qb}", "reps0"), wr=(ak,))
                if dbg and si == 0 and qb == 0 and j == 0:
                    dump(2, aslot, (ak,), f32=True)
                    dump(7, w1[:, 0:1024], ("w1res0",))
                st = j % 2
                ln_stats_v(aslot, (ak,), st)
                ln_apply(aslot, aslot, (ak,), (ak,), st)
                xslot = xs[:, st * 1024:st * 1024 + 1024]
                xk = f"xs{st}"
                DMA("sync", lambda e, tt=t, d=xslot: e.dma_start(
                    out=d, in_=d_x[si, tt * 128:(tt + 1) * 128, :]), rd=(), wr=(xk,))
                V(lambda e, a=aslot, x_=xslot: e.tensor_add(a, a, x_),
                  rd=(ak, xk), wr=(ak,))
                DMA("sync", lambda e, tt=t, a=aslot: e.dma_start(
                    out=d_u[si, tt * 128:(tt + 1) * 128, :], in_=a),
                    rd=(ak,), wr=(f"u_{t}",))

        if upto == "A":
            if dbg and si == 1:
                dump(6, w1[:, 0:1024], ("w1res0",))
            for t in range(NT):
                DMA("sync", lambda e, tt=t: e.dma_start(
                    out=xs[:, 0:1024], in_=d_u[si, tt * 128:(tt + 1) * 128, :]),
                    rd=(f"u_{t}",), wr=("xs0",))
                DMA("sync", lambda e, tt=t: e.dma_start(
                    out=d_out[si, tt * 128:(tt + 1) * 128, :], in_=xs[:, 0:1024]),
                    rd=("xs0",), wr=(f"outd_{si}_{t}",))
            sch.fence((), fm_keys("h0") + qk_keys("q") + qk_keys("k") +
                      tuple(f"v_{t}_{o}" for t in range(NT) for o in range(2)) +
                      ("w1_0", "w1_1") +
                      tuple(f"pt{k}" for k in range(NT)) +
                      tuple(f"cx{d}" for d in range(8)))
            return None

        sch.fence(tuple(f"pt{k}" for k in range(NT)) +
                  tuple(f"cx{d}" for d in range(8)), fm_keys("g"))
        sch.fence(tuple(f"v_{t}_{o}" for t in range(NT) for o in range(2)) +
                  qk_keys("q") + qk_keys("k"),
                  ("arVnew", "arQnew", "arKnew"))
        for t in range(NT):
            xslot = xs[:, (t % 2) * 1024:(t % 2) * 1024 + 1024]
            xk = f"xs{t % 2}"
            DMA("sync", lambda e, tt=t, d=xslot: e.dma_start(
                out=d, in_=d_u[si, tt * 128:(tt + 1) * 128, :]),
                rd=(f"u_{t}",), wr=(xk,))
            st = t % 2
            ln_stats(xslot, (xk,), st)
            tslot = tma[:, st * 1024:st * 1024 + 1024]
            tk = f"tma{st}"
            ln_apply(tslot, xslot, (xk,), (tk,), st)
            tm_to_fm(tslot, arX, t, (tk,), "g")
        sch.fence(("bifm", "bvB", "arVnew", "arQnew", "arKnew"),
                  ("biasfm", "bvrow"))
        sch.fence(tuple(f"w1res{d}" for d in range(8)), ("w1_0", "w1_1"))
        for vog in range(2):
            wk = f"w1_{vog}"
            for dc in range(8):
                DMA("gpsimd", lambda e, d=dc, o=vog: e.dma_start(
                    out=w1[:, o * 4096 + d * 512: o * 4096 + (d + 1) * 512],
                    in_=d_winproj[2, o, d]),
                    rd=(), wr=(wk,))
            for t in range(NT):
                pc = psC[t % 2]
                T(mm(pc[:, :], ones_row,
                     vrow[0:1, 1024 + vog * 512: 1024 + vog * 512 + 512],
                     True, False),
                  rd=("vrowc", "bvrow"), wr=(f"psC{t % 2}",))
                for dc in range(8):
                    T(mm(pc[:, :], arX[:, dc * S + t * 128: dc * S + t * 128 + 128],
                         w1[:, vog * 4096 + dc * 512: vog * 4096 + (dc + 1) * 512],
                         False, dc == 7),
                      rd=(wk,) + fm_keys("g", dc, [t]), wr=(f"psC{t % 2}",))
                V(lambda e, tt=t, o=vog, p=pc: e.tensor_copy(
                    arV[:, tt * DM + o * 512: tt * DM + o * 512 + 512], p[:, :]),
                  rd=(f"psC{t % 2}",), wr=(f"v_{t}_{vog}",))

        sch.fence(fm_keys("g"), fm_keys("r"))
        for t in range(NT):
            xslot = xs[:, (t % 2) * 1024:(t % 2) * 1024 + 1024]
            xk = f"xs{t % 2}"
            DMA("sync", lambda e, tt=t, d=xslot: e.dma_start(
                out=d, in_=d_u[si, tt * 128:(tt + 1) * 128, :]),
                rd=(f"u_{t}",), wr=(xk,))
            st = t % 2
            ln_stats(xslot, (xk,), st)
            gslot = tma[:, 0:1024]
            ln_apply(gslot, xslot, (xk,), ("tma0",), st)
            ce = tbl[:, 0:512]
            se = tbl[:, 512:1024]
            DMA("sync", lambda e, tt=t, d=ce: e.dma_start(
                out=d, in_=d_ce[tt * 128:(tt + 1) * 128, :]), rd=("tbl0",), wr=("tbl0w",))
            DMA("sync", lambda e, tt=t, d=se: e.dma_start(
                out=d, in_=d_se[tt * 128:(tt + 1) * 128, :]), rd=("tbl1",), wr=("tbl1w",))
            rslot = tma[:, 1024:2048]
            t1 = tbl[:, 1024:1536]
            t2 = tbl[:, 1536:2048]
            V(lambda e, g=gslot, c=ce: e.tensor_mul(t1, g[:, 0:1024:2], c),
              rd=("tma0", "tbl0w"), wr=("rt1",))
            V(lambda e, g=gslot, s_=se: e.tensor_mul(t2, g[:, 1:1024:2], s_),
              rd=("tma0", "tbl1w"), wr=("rt2",))
            V(lambda e, r=rslot: e.tensor_sub(r[:, 0:1024:2], t1, t2),
              rd=("rt1", "rt2"), wr=("tma1e",))
            V(lambda e, g=gslot, s_=se: e.tensor_mul(t1, g[:, 0:1024:2], s_),
              rd=("tma0", "tbl1w", "tma1e"), wr=("rt1",))
            V(lambda e, g=gslot, c=ce: e.tensor_mul(t2, g[:, 1:1024:2], c),
              rd=("tma0", "tbl0w", "tma1e"), wr=("rt2",))
            V(lambda e, r=rslot: e.tensor_add(r[:, 1:1024:2], t1, t2),
              rd=("rt1", "rt2"), wr=("tma1o",))
            sch.fence(("tbl0w",), ("tbl0",))
            sch.fence(("tbl1w",), ("tbl1",))
            tm_to_fm(rslot, arX, t, ("tma1e", "tma1o"), "r")
        for part, dsth, pref in ((0, arQ, "q"), (1, arK, "k")):
            for og in range(2):
                wk = f"w1_{og}"
                for dc in range(8):
                    DMA("gpsimd", lambda e, p=part, d=dc, o=og: e.dma_start(
                        out=w1[:, o * 4096 + d * 512: o * 4096 + (d + 1) * 512],
                        in_=d_winproj[p, o, d]),
                        rd=(), wr=(wk,))
                for tc in range(4):
                    for oc4 in range(4):
                        oc = og * 4 + oc4
                        un, hh = oc4 // 2, oc4 % 2
                        for dc in range(8):
                            T(mm(psS[un][:, hh * 512:hh * 512 + 512],
                                 w1[:, og * 4096 + dc * 512 + oc4 * 128:
                                    og * 4096 + dc * 512 + oc4 * 128 + 128],
                                 arX[:, dc * S + tc * 512: dc * S + tc * 512 + 512],
                                 dc == 0, dc == 7),
                              rd=(wk,) + fm_keys("r", dc, range(tc * 4, tc * 4 + 4)),
                              wr=(f"psS{un}_{hh}",))
                        A_(lambda e, dh=dsth, o=oc, c=tc, u=un, h=hh,
                           b=bifm[:, part * 8 + oc:part * 8 + oc + 1]:
                           e.activation(dh[:, o * S + c * 512: o * S + c * 512 + 512],
                                        psS[u][:, h * 512:h * 512 + 512],
                                        AF.Identity, bias=b),
                           rd=(f"psS{un}_{hh}", "biasfm"),
                           wr=(f"{pref}_{oc}_tc{tc}",))

        def emit_norms_B():
            for which, pref, src, bcol in (("q", "q", arQ, 608), ("k", "k", arK, 672)):
                for pc in range(8):
                    for tc in range(4):
                        V(lambda e, s_=src, p=pc, c=tc: e.tensor_mul(
                            q2b, s_[:, p * S + c * 512: p * S + c * 512 + 512],
                            s_[:, p * S + c * 512: p * S + c * 512 + 512]),
                          rd=(f"{pref}_{pc}_tc{tc}",), wr=("q2b", "dsink"))
                        T(mm(psL[0:1, 0:512], onescol[:, 0:1], q2b,
                             True, True),
                          rd=("q2b", "dsink", "onescol"), wr=("psL",))
                        if which == "q":
                            for h in range(2):
                                V(lambda e, p=pc, c=tc, hh=h: e.reduce_max(
                                    scr[0:1, 608 + p * 8 + c * 2 + hh:
                                        609 + p * 8 + c * 2 + hh],
                                    psL[0:1, hh * 256:hh * 256 + 256], axis=AX),
                                  rd=("psL",), wr=("nbq",))
                        else:
                            V(lambda e, p=pc, c=tc: e.reduce_max(
                                scr[0:1, 672 + p * 4 + c: 673 + p * 4 + c],
                                psL[0:1, 0:512], axis=AX),
                              rd=("psL",), wr=("nbk",))
            DMA("sync", lambda e: e.dma_start(out=d_mb[0:96],
                                              in_=scr[0:1, 608:704]),
                rd=("nbq", "nbk"), wr=("mbd",))
            DMA("sync", lambda e: e.dma_start(
                out=sc2[0:8, 123:131], in_=d_mb[0:64].rearrange("(p c) -> p c", p=8)),
                rd=("mbd",), wr=("qnB",))
            DMA("sync", lambda e: e.dma_start(
                out=sc2[0:8, 131:135], in_=d_mb[64:96].rearrange("(p c) -> p c", p=8)),
                rd=("mbd",), wr=("knB",))
            V(lambda e: e.reduce_max(sc2[0:8, 135:136], sc2[0:8, 131:135], axis=AX),
              rd=("knB",), wr=("knBm",))
            V(lambda e: e.tensor_scalar(sc2[0:8, 136:144], sc2[0:8, 123:131],
                                        sc2[0:8, 135:136], SCB * SCB, OP.mult, OP.mult),
              rd=("qnB", "knBm"), wr=("mtB",))
            A_(lambda e: e.activation(sc2[0:8, 136:144], sc2[0:8, 136:144], AF.Ln),
               rd=("mtB",), wr=("mtB",))
            A_(lambda e: e.activation(sc2[0:8, 136:144], sc2[0:8, 136:144], AF.Exp,
                                      scale=0.5), rd=("mtB",), wr=("mtB",))
            V(lambda e: e.tensor_scalar(sc2[0:8, 136:144], sc2[0:8, 136:144], -1.0,
                                        None, OP.mult), rd=("mtB",), wr=("mtB",))
            DMA("sync", lambda e: e.dma_start(out=d_mb[0:64], in_=sc2[0:8, 136:144]),
                rd=("mtB",), wr=("mbd",))
            DMA("sync", lambda e: e.dma_start(out=scr[0:1, 608:672],
                                              in_=d_mb[0:64]),
                rd=("mbd",), wr=("negmB",))
            V(lambda e: e.tensor_copy(mrow[0:1, 0:64], scr[0:1, 608:672]),
              rd=("negmB",), wr=("negmBb",))
            for j in range(64):
                T(mm(psL[:, j:j + 1], ones_row, mrow[0:1, j:j + 1], True, True),
                  rd=("negmBb", "vrowc"), wr=("psL",))
            V(lambda e: e.tensor_copy(scr[:, 12:76], psL[:, 0:64]),
              rd=("psL",), wr=("biasB",))

        emit_norms_B()

        sch.fence(("w1_0", "w1_1"), tuple(f"w1res{d}" for d in range(8)))
        for dc in range(8):
            DMA("gpsimd", lambda e, d=dc: e.dma_start(
                out=w1[:, d * DM:(d + 1) * DM], in_=d_woutproj[d]),
                rd=(), wr=(f"w1res{dc}",))

        if upto == "bqkv":
            for t in range(NT):
                DMA("sync", lambda e, tt=t: e.dma_start(
                    out=xs[:, 0:1024], in_=d_u[si, tt * 128:(tt + 1) * 128, :]),
                    rd=(f"u_{t}",), wr=("xs0",))
                DMA("sync", lambda e, tt=t: e.dma_start(
                    out=d_out[si, tt * 128:(tt + 1) * 128, :], in_=xs[:, 0:1024]),
                    rd=("xs0",), wr=(f"outd_{si}_{t}",))
            sch.fence((), tuple(f"pt{k}" for k in range(NT)) +
                      tuple(f"cx{d}" for d in range(8)) +
                      fm_keys("h0") + fm_keys("g") + fm_keys("r") +
                      qk_keys("q") + qk_keys("k") +
                      tuple(f"v_{t}_{o}" for t in range(NT) for o in range(2)) +
                      ("w1_0", "w1_1") + tuple(f"w1res{d}" for d in range(8)))
            return None

        sch.fence(fm_keys("r"), tuple(f"pt{k}" for k in range(NT)) +
                  tuple(f"cx{d}" for d in range(8)))
        for qb in range(BATTN["nqb"]):
            nkt = 2 * qb + 2
            for pc in range(8):
                for ku in range(nkt // 2):
                    un = ku % 2
                    for ki in range(2):
                        kt = ku * 2 + ki
                        n0 = 128 if kt == 2 * qb + 1 else 0
                        for hh in range(2):
                            T(mm(psS[un][:, hh * 512 + ki * 256 + n0:
                                         hh * 512 + ki * 256 + 256],
                                 arK[hh * 64:hh * 64 + 64,
                                     pc * S + kt * 128: pc * S + kt * 128 + 128],
                                 arQ[hh * 64:hh * 64 + 64,
                                     pc * S + qb * 256 + n0: pc * S + qb * 256 + 256],
                                 True, True, tp=(hh * 64, 0)),
                              rd=(f"k_{pc}_tc{kt // 4}", f"q_{pc}_tc{qb // 2}"),
                              wr=(f"psS{un}_{hh}",))
                    in3 = psS[un][:, :].rearrange("p (h q) -> p h q", h=2)
                    out3 = arX[:, :].rearrange("p (h z) -> p h z", h=4)[
                        :, 0:2, ku * 512: ku * 512 + 512]
                    A_(lambda e, i3=in3, o3=out3, p=pc, q=qb: e.activation(
                        o3, i3, AF.Exp,
                        bias=scr[:, 12 + p * 8 + q: 13 + p * 8 + q], scale=SCB),
                       rd=(f"psS{un}_0", f"psS{un}_1", "biasB"),
                       wr=(f"pt{2 * ku}", f"pt{2 * ku + 1}"))
                ktd = 2 * qb
                for hh in range(2) if BATTN["mask"] else []:
                    G(lambda e, k=ktd, h2=hh: e.tensor_mul(
                        arX[:, h2 * 4096 + k * 256: h2 * 4096 + k * 256 + 128],
                        arX[:, h2 * 4096 + k * 256: h2 * 4096 + k * 256 + 128],
                        trimask[:, :]),
                      rd=(f"pt{ktd}", "trimask"), wr=(f"pt{ktd}",))
                    G(lambda e, k=ktd + 1, h2=hh: e.memset(
                        arX[:, h2 * 4096 + k * 256: h2 * 4096 + k * 256 + 128], 0.0),
                      rd=(), wr=(f"pt{ktd + 1}",))
                    G(lambda e, k=ktd + 1, h2=hh: e.tensor_mul(
                        arX[:, h2 * 4096 + k * 256 + 128:
                            h2 * 4096 + k * 256 + 256],
                        arX[:, h2 * 4096 + k * 256 + 128:
                            h2 * 4096 + k * 256 + 256],
                        trimask[:, :]),
                      rd=(f"pt{ktd + 1}", "trimask"), wr=(f"pt{ktd + 1}",))
                for kt in range(nkt) if BATTN["lsum"] else []:
                    T(mm(psL[0:1, 0:256], onescol[:, 0:1],
                         arX[:, kt * 256: kt * 256 + 256], kt == 0, kt == nkt - 1),
                      rd=(f"pt{kt}", "onescol"), wr=("psL",))
                    T(mm(psL[0:1, 256:512], onescol[:, 0:1],
                         arX[:, 4096 + kt * 256: 4096 + kt * 256 + 256],
                         False, kt == nkt - 1),
                      rd=(f"pt{kt}", "onescol"), wr=("psL",))
                if not BATTN["lsum"]:
                    continue
                cp = psC[pc % 2]
                for kt in range(nkt) if BATTN["ctx"] else []:
                    for hh in range(2):
                        T(mm(cp[hh * 64:hh * 64 + 64, 0:256],
                             arV[:, kt * DM + (2 * pc + hh) * 64:
                                 kt * DM + (2 * pc + hh) * 64 + 64],
                             arX[:, hh * 4096 + kt * 256:
                                 hh * 4096 + kt * 256 + 256],
                             kt == 0, kt == nkt - 1, tp=(0, hh * 64)),
                          rd=(f"v_{kt}_{(2 * pc + hh) // 8}", f"pt{kt}"),
                          wr=(f"psC{pc % 2}",))
                V(lambda e: e.tensor_copy(vrow[0:1, 2432:2944], psL[0:1, 0:512]),
                  rd=("psL",), wr=("rrowb",))
                T(mm(psL[:, 0:256], maskA_row, vrow[0:1, 2432:2688], True, False),
                  rd=("rrowb", "vrowc"), wr=("psL",))
                T(mm(psL[:, 0:256], maskB_row, vrow[0:1, 2688:2944], False, True),
                  rd=("rrowb", "vrowc"), wr=("psL",))
                V(lambda e: e.reciprocal(rbb[:, :], psL[:, 0:256]),
                  rd=("psL",), wr=("rbb",))
                if BATTN["ctx"]:
                    V(lambda e, p=pc, c=cp: e.tensor_mul(
                        arX[:, 8192 + p * 256: 8192 + (p + 1) * 256], c[:, 0:256],
                        rbb[:, :]),
                      rd=(f"psC{pc % 2}", "rbb"), wr=(f"cx{pc}",))
            if BATTN["epi"]:
                for j in range(2):
                    t = qb * 2 + j
                    st = j % 2
                    xslot = xs[:, st * 1024:st * 1024 + 1024]
                    xk = f"xs{st}"
                    DMA("sync", lambda e, tt=t, d=xslot: e.dma_start(
                        out=d, in_=d_u[si, tt * 128:(tt + 1) * 128, :]),
                        rd=(f"u_{t}",), wr=(xk,))
                    ln_stats_v(xslot, (xk,), st)
                    gslot = fsb[:, st * 1024:st * 1024 + 1024]
                    ln_apply(gslot, xslot, (xk,), (f"fs{st}",), st)
            for j in range(2) if BATTN["epi"] else []:
                t = qb * 2 + j
                un = j
                for dc in range(8):
                    for vc in range(2):
                        T(mm(psS[un][:, vc * 512:vc * 512 + 512],
                             arX[:, 8192 + dc * 256 + j * 128:
                                 8192 + dc * 256 + j * 128 + 128],
                             w1[:, dc * DM + vc * 512: dc * DM + vc * 512 + 512],
                             dc == 0, dc == 7),
                          rd=(f"cx{dc}", f"w1res{dc}"), wr=(f"psS{un}_{vc}",))
                st = j % 2
                gslot = fsb[:, st * 1024:st * 1024 + 1024]
                gk = f"fs{st}"
                V(lambda e, g=gslot, u=un: e.scalar_tensor_tensor(
                    g, psS[u][:, :], 1.0, g, OP.mult, OP.add),
                  rd=(f"psS{un}_0", f"psS{un}_1", gk), wr=(gk,))
                V(lambda e, g=gslot: e.tensor_add(g, g, reps[:, DM:2 * DM]),
                  rd=(gk, "reps1"), wr=(gk,))
                ln_stats_v(gslot, (gk,), st)
                ln_apply(gslot, gslot, (gk,), (gk,), st)
                V(lambda e, g=gslot: e.tensor_mul(g, g, reps[:, 2 * DM:3 * DM]),
                  rd=(gk, "reps2"), wr=(gk,))
                yslot = y1s[:, st * 1024:st * 1024 + 1024]
                yk = f"y1s{st}"
                V(lambda e, y=yslot, g=gslot: e.tensor_add(
                    y, g, reps[:, 3 * DM:4 * DM]),
                  rd=(gk, "reps3"), wr=(yk,))
                DMA("sync", lambda e, tt=t, y=yslot: e.dma_start(
                    out=d_y1[tt * 128:(tt + 1) * 128, :], in_=y),
                    rd=(yk,), wr=(f"y1d_{t}",))

        if upto == "battn":
            for t in range(NT):
                DMA("gpsimd", lambda e, tt=t: e.dma_start(
                    out=xs[:, 0:1024], in_=d_y1[tt * 128:(tt + 1) * 128, :]),
                    rd=(f"y1d_{t}",), wr=("xs0",))
                DMA("sync", lambda e, tt=t: e.dma_start(
                    out=d_out[si, tt * 128:(tt + 1) * 128, :], in_=xs[:, 0:1024]),
                    rd=("xs0",), wr=(f"outd_{si}_{t}",))
            sch.fence((), tuple(f"pt{k}" for k in range(NT)) +
                      tuple(f"cx{d}" for d in range(8)) +
                      fm_keys("h0") + fm_keys("g") + fm_keys("r") +
                      qk_keys("q") + qk_keys("k") +
                      tuple(f"v_{t}_{o}" for t in range(NT) for o in range(2)) +
                      ("w1_0", "w1_1") + tuple(f"w1res{d}" for d in range(8)))
            return None

        sch.fence(tuple(f"pt{k}" for k in range(NT)) +
                  tuple(f"cx{d}" for d in range(8)), fm_keys("yf"))
        sch.fence(qk_keys("q"), tuple(f"moe_{t}" for t in range(NT)))
        sch.fence(tuple(f"v_{t}_{o}" for t in range(NT) for o in range(2)),
                  ("esl0", "esl1"))
        for t in range(NT):
            st = t % 2
            yslot = y1s[:, st * 1024:st * 1024 + 1024]
            yk = f"y1s{st}"
            DMA("sync", lambda e, tt=t, y=yslot: e.dma_start(
                out=y, in_=d_y1[tt * 128:(tt + 1) * 128, :]),
                rd=(f"y1d_{t}",), wr=(yk,))
            tm_to_fm(yslot, arX, t, (yk,), "yf")
            for dc in range(8):
                T(mm(psL[0:8, 0:128], gwt[:, dc * 8:(dc + 1) * 8],
                     arX[:, dc * S + t * 128: dc * S + t * 128 + 128],
                     dc == 0, dc == 7),
                  rd=fm_keys("yf", dc, [t]) + ("gwt",), wr=("psL",))
            A_(lambda e: e.activation(scr[0:8, 720:848], psL[0:8, 0:128],
                                      AF.Identity, bias=gbb[0:8, 0:1]),
               rd=("psL", "gbb"), wr=("glog",))
            T(lambda e: e.transpose(psL[:, 0:8], scr[0:8, 720:848], identf[0:8, 0:8]),
              rd=("glog", "identf"), wr=("psL",))
            gcols = gts[:, t * 16: t * 16 + 8]
            wcols = gts[:, t * 16 + 8: t * 16 + 16]
            gk2, wk2 = f"g8_{t}", f"w8_{t}"
            V(lambda e, g=gcols: e.tensor_copy(g, psL[:, 0:8]),
              rd=("psL",), wr=(gk2,))
            V(lambda e, g=gcols: e.reduce_max(sc2[:, 96:97], g, axis=AX),
              rd=(gk2,), wr=("gmax",))
            V(lambda e: e.tensor_scalar(sc2[:, 97:98], sc2[:, 96:97], -1.0, None,
                                        OP.mult), rd=("gmax",), wr=("gnm",))
            A_(lambda e, g=gcols: e.activation(g, g, AF.Exp, bias=sc2[:, 97:98]),
               rd=(gk2, "gnm"), wr=(gk2,))
            V(lambda e, g=gcols: e.reduce_sum(sc2[:, 98:99], g, axis=AX),
              rd=(gk2,), wr=("gsum",))
            V(lambda e: e.reciprocal(sc2[:, 99:100], sc2[:, 98:99]),
              rd=("gsum",), wr=("grcp",))
            V(lambda e, g=gcols: e.tensor_scalar(g, g, sc2[:, 99:100], None, OP.mult),
              rd=(gk2, "grcp"), wr=(gk2,))
            V(lambda e, g=gcols: e.reduce_max(sc2[:, 100:101], g, axis=AX),
              rd=(gk2,), wr=("gm1",))
            V(lambda e, g=gcols: e.tensor_scalar(scr[:, 704:712], g, sc2[:, 100:101],
                                                 None, OP.is_ge),
              rd=(gk2, "gm1"), wr=("gmask",))
            V(lambda e, g=gcols: e.scalar_tensor_tensor(
                scr[:, 712:720], scr[:, 704:712], -1e30, g, OP.mult, OP.add),
              rd=("gmask", gk2), wr=("gmasked",))
            V(lambda e: e.reduce_max(sc2[:, 101:102], scr[:, 712:720], axis=AX),
              rd=("gmasked",), wr=("gm2",))
            V(lambda e, g=gcols: e.tensor_scalar(scr[:, 704:712], g, sc2[:, 101:102],
                                                 None, OP.is_ge),
              rd=(gk2, "gm2"), wr=("gmask",))
            V(lambda e, g=gcols, w=wcols: e.tensor_mul(w, g, scr[:, 704:712]),
              rd=(gk2, "gmask"), wr=(wk2,))
            V(lambda e, w=wcols: e.tensor_copy(wg8[:, 0:8], w),
              rd=(wk2,), wr=("wg8",))
            T(lambda e: e.transpose(ptr[0:8, 0:128], wg8[:, 0:8], identb[:, :]),
              rd=("wg8", "identb"), wr=("ptr0",))
            V(lambda e: e.tensor_copy(mrow[0:8, 0:128], ptr[0:8, 0:128]),
              rd=("ptr0",), wr=("wg8T",))
            for vc in range(2):
                T(mm(psS[1][:, vc * 512:vc * 512 + 512], mrow[0:8, 0:128],
                     geb[0:8, vc * 512:vc * 512 + 512], True, True),
                  rd=("wg8T", "geb"), wr=(f"psS1_{vc}",))
            V(lambda e, tt=t: e.tensor_copy(
                arQ[:, tt * DM:(tt + 1) * DM], psS[1][:, :]),
              rd=("psS1_0", "psS1_1"), wr=(f"moe_{t}",))
        for ei in range(NE):
            eoff = (ei % 2) * 8192
            ek = f"esl{ei % 2}"
            DMA("gpsimd", lambda e, ee=ei, off=eoff: e.dma_start(
                out=arV[:, off:off + 8192], in_=d_wexp[ee]),
                rd=(), wr=(ek,))
            for t in range(NT):
                un = t % 2
                for dc in range(8):
                    for fh in range(2):
                        T(mm(psS[un][:, fh * 512:fh * 512 + 512],
                             arX[:, dc * S + t * 128: dc * S + t * 128 + 128],
                             arV[:, eoff + dc * DM + fh * 512:
                                 eoff + dc * DM + fh * 512 + 512],
                             dc == 0, dc == 7),
                          rd=(ek,) + fm_keys("yf", dc, [t]),
                          wr=(f"psS{un}_{fh}",))
                for fh in range(2):
                    A_(lambda e, u=un, f=fh, tt=t, ee=ei: e.activation(
                        etmp[:, f * 512:f * 512 + 512], psS[u][:, f * 512:f * 512 + 512],
                        AF.Identity, scale=gts[:, tt * 16 + 8 + ee:tt * 16 + 9 + ee]),
                       rd=(f"psS{un}_{fh}", f"w8_{t}"), wr=(f"etmp{fh}",))
                    V(lambda e, f=fh, tt=t: e.tensor_add(
                        arQ[:, tt * DM + f * 512: tt * DM + f * 512 + 512],
                        arQ[:, tt * DM + f * 512: tt * DM + f * 512 + 512],
                        etmp[:, f * 512:f * 512 + 512]),
                      rd=(f"etmp{fh}", f"moe_{t}"), wr=(f"moe_{t}",))

        def finals_tile(t):
            st = 2 + t % 2
            yslot = y1s[:, (t % 2) * 1024:(t % 2) * 1024 + 1024]
            yk = f"y1s{t % 2}"
            DMA("sync", lambda e, tt=t, y=yslot: e.dma_start(
                out=y, in_=d_y1[tt * 128:(tt + 1) * 128, :]),
                rd=(f"y1d_{t}",), wr=(yk,))
            fslot = fsb[:, (t % 2) * 1024:(t % 2) * 1024 + 1024]
            fk = f"fs{t % 2}"
            V(lambda e, f=fslot, y=yslot, tt=t: e.tensor_add(
                f, y, arQ[:, tt * DM:(tt + 1) * DM]),
              rd=(yk, f"moe_{t}"), wr=(fk,))
            ln_stats_v(fslot, (fk,), st)
            ln_apply(fslot, fslot, (fk,), (fk,), st)
            V(lambda e, f=fslot: e.tensor_mul(f, f, reps[:, 4 * DM:5 * DM]),
              rd=(fk, "reps4"), wr=(fk,))
            V(lambda e, f=fslot: e.tensor_add(f, f, reps[:, 5 * DM:6 * DM]),
              rd=(fk, "reps5"), wr=(fk,))
            ln_stats_v(fslot, (fk,), st)
            ln_apply(fslot, fslot, (fk,), (fk,), st)
            xslot = xs[:, (t % 2) * 1024:(t % 2) * 1024 + 1024]
            xk = f"xs{t % 2}"
            DMA("sync", lambda e, tt=t, d=xslot: e.dma_start(
                out=d, in_=d_u[si, tt * 128:(tt + 1) * 128, :]),
                rd=(f"u_{t}",), wr=(xk,))
            V(lambda e, f=fslot, x_=xslot: e.tensor_add(f, f, x_),
              rd=(fk, xk), wr=(fk,))
            DMA("sync", lambda e, tt=t, f=fslot: e.dma_start(
                out=d_out[si, tt * 128:(tt + 1) * 128, :], in_=f),
                rd=(fk,), wr=(f"outd_{si}_{t}",))

        def finals_finish():
            sch.fence(tuple(f"moe_{t}" for t in range(NT)), qk_keys("q"))

        sch.fence(fm_keys("yf") + ("esl0", "esl1") +
                  tuple(f"w1res{d}" for d in range(8)),
                  fm_keys("h0") + qk_keys("k") +
                  tuple(f"v_{t}_{o}" for t in range(NT) for o in range(2)) +
                  ("w1_0", "w1_1"))
        return finals_tile, finals_finish

    tail = None
    for si in range(NSEQ):
        tail = emit_seq(si, tail_cb=tail)
    if tail is not None and upto == "full":
        ft, ff = tail
        for t in range(NT):
            ft(t)
        ff()

    allout = [k for k in sch.last_w if k.startswith("outd_")]
    sch.add("sync", None, rd=tuple(allout), wr=())

    n = sch.finalize(nc, es)
    es.close()
    return nc, n


_BUILD_CACHE = {}


def _stage(inputs):
    f32 = np.float32
    bf = ml_dtypes.bfloat16

    def c(a):
        return np.ascontiguousarray(a, dtype=f32)

    st = {}
    st["wqkv"] = c(np.asarray(inputs["L0_attn_W_qkv"], f32)
                   .reshape(8, 128, 6, 512).transpose(2, 0, 1, 3))
    b_qkv = np.asarray(inputs["L0_attn_b_qkv"], f32)
    st["bqkv_fm"] = c(b_qkv.reshape(24, 128).T)
    st["bqkv_row"] = c(b_qkv.reshape(1, 3072))
    st["wout"] = c(np.asarray(inputs["L0_attn_W_out"], f32).reshape(8, 128, DM))
    st["bout_rep"] = c(np.broadcast_to(np.asarray(inputs["L0_attn_b_out"], f32),
                                       (128, DM)))
    ip = np.asarray(inputs["L0_ffn_in_proj_w"], f32)
    st["winproj"] = c(np.stack([
        ip[i * DM:(i + 1) * DM].T.reshape(8, 128, 2, 512).transpose(2, 0, 1, 3)
        for i in range(3)]))
    bip = np.asarray(inputs["L0_ffn_in_proj_b"], f32)
    st["binproj_fm"] = c(bip.reshape(24, 128).T)
    st["binproj_row"] = c(bip.reshape(1, 3072))
    st["woutproj"] = c(np.asarray(inputs["L0_ffn_out_proj_w"], f32).T
                       .reshape(8, 128, DM))
    st["boutproj_rep"] = c(np.broadcast_to(
        np.asarray(inputs["L0_ffn_out_proj_b"], f32), (128, DM)))
    gw = np.asarray(inputs["L0_ffn_gate_w"], f32).T
    st["wgate"] = c(gw.reshape(8, 128, 8).transpose(1, 0, 2).reshape(128, 64))
    st["bgate"] = c(np.asarray(inputs["L0_ffn_gate_b"], f32).reshape(8, 1))
    we = np.asarray(inputs["L0_ffn_expert_w"], f32)
    st["wexp"] = c(np.stack([
        we[e].T.reshape(8, 128, DM).transpose(1, 0, 2).reshape(128, 8 * DM)
        for e in range(NE)]))
    st["bexp"] = c(np.asarray(inputs["L0_ffn_expert_b"], f32))
    st["ln1w_rep"] = c(np.broadcast_to(np.asarray(inputs["L0_ffn_ln1_w"], f32),
                                       (128, DM)))
    st["ln1b_rep"] = c(np.broadcast_to(np.asarray(inputs["L0_ffn_ln1_b"], f32),
                                       (128, DM)))
    st["ln2w_rep"] = c(np.broadcast_to(np.asarray(inputs["L0_ffn_ln2_w"], f32),
                                       (128, DM)))
    st["ln2b_rep"] = c(np.broadcast_to(np.asarray(inputs["L0_ffn_ln2_b"], f32),
                                       (128, DM)))
    st["identb"] = np.eye(128, dtype=bf)
    st["identf"] = np.eye(128, dtype=f32)
    st["onescol"] = np.ones((128, 1), dtype=bf)
    rowc = np.zeros((1, 384), dtype=bf)
    rowc[0, 0:128] = 1.0
    rowc[0, 128:192] = 1.0
    rowc[0, 320:384] = 1.0
    st["rowconst"] = rowc
    st["trimask"] = np.triu(np.ones((128, 128), dtype=f32)).astype(bf)
    inv = (1.0 / (10000.0 ** (np.arange(0, HD, 2, dtype=f32) / f32(HD)))).astype(f32)
    fr = (np.arange(S, dtype=f32)[:, None] * inv[None, :]).astype(f32)
    st["ce"] = c(np.tile(np.cos(fr).astype(f32), (1, NH)))
    st["se"] = c(np.tile(np.sin(fr).astype(f32), (1, NH)))
    return st


def kernel(**inputs):
    if "full" not in _BUILD_CACHE:
        _BUILD_CACHE["full"] = build_nc("full")
    nc, _ = _BUILD_CACHE["full"]
    st = _stage(inputs)
    x = np.ascontiguousarray(np.asarray(inputs["L0_attn_x"], np.float32))
    in_maps = []
    for ci in range(NCORES):
        m = dict(st)
        m["x"] = x[ci * NSEQ:(ci + 1) * NSEQ]
        in_maps.append(m)
    res = run_bass_kernel_spmd(nc, in_maps, core_ids=list(range(NCORES)))
    out = np.concatenate([np.asarray(r["out"]) for r in res.results], axis=0)
    return out.reshape(16, S, DM).astype(np.float32)


# version 08
# speedup vs baseline: 1.0165x; 1.0165x over previous
import contextlib

import numpy as np
import ml_dtypes

import concourse.bass as bass
import concourse.mybir as mybir
from concourse.bass_utils import run_bass_kernel_spmd

F32 = mybir.dt.float32
BF16 = mybir.dt.bfloat16
AF = mybir.ActivationFunctionType
OP = mybir.AluOpType
AX = mybir.AxisListType.X

NCORES = 8
NSEQ = 2
S = 2048
DM = 1024
NT = 16
NH = 16
HD = 64
QB = 256
NQB = S // QB
NE = 8
BATTN = dict(nqb=8, lsum=True, bcast=True, ctx=True, mask=True, epi=True)
EPS = 1e-5
SCA = 1.0 / 32.0
SCB = 1.0 / 8.0

ENGINES = ("tensor", "vector", "scalar", "gpsimd", "sync")
DMA_QUEUES = ("sync", "gpsimd", "scalar")
DMA_POOL = 8


class Sched:

    def __init__(self):
        self.ops = []
        self.last_w = {}
        self.readers = {}
        self.ecount = {}

    def add(self, eng, fn, rd=(), wr=(), dma=False):
        oid = len(self.ops)
        deps = set()
        for k in rd:
            w = self.last_w.get(k)
            if w is not None:
                deps.add(w)
        for k in wr:
            w = self.last_w.get(k)
            if w is not None:
                deps.add(w)
            for r in self.readers.get(k, ()):
                deps.add(r)
        for k in rd:
            self.readers.setdefault(k, []).append(oid)
        for k in wr:
            self.last_w[k] = oid
            self.readers[k] = []
        eseq = self.ecount.get(eng, 0)
        self.ecount[eng] = eseq + 1
        self.ops.append(dict(eng=eng, fn=fn, dma=dma, deps=deps, inc=False,
                             eseq=eseq))
        return oid

    def fence(self, rd, wr):
        self.add("vector", lambda e: e.engine_nop(), rd=tuple(rd), wr=tuple(wr))

    def finalize(self, nc, es):
        for o in self.ops:
            kept = set()
            for d in o["deps"]:
                p = self.ops[d]
                if (p["eng"] == o["eng"] and not p["dma"] and not o["dma"]):
                    if p["eng"] == "tensor":
                        continue
                    if o["eseq"] - p["eseq"] > 2:
                        continue
                kept.add(d)
            o["deps"] = kept
        for o in self.ops:
            for d in o["deps"]:
                self.ops[d]["inc"] = True
        for o in self.ops:
            if o["dma"]:
                o["inc"] = True
        csem = {e: es.enter_context(nc.semaphore(f"c_{e}")) for e in ENGINES}
        dsem = {q: [es.enter_context(nc.semaphore(f"d_{q}_{i}"))
                    for i in range(DMA_POOL)] for q in DMA_QUEUES}
        ccount = {e: 0 for e in ENGINES}
        dcount = {q: 0 for q in DMA_QUEUES}
        slot_uses = {q: [0] * DMA_POOL for q in DMA_QUEUES}
        for o in self.ops:
            if o["dma"]:
                q = o["eng"]
                slot = dcount[q] % DMA_POOL
                dcount[q] += 1
                o["sem"] = dsem[q][slot]
                o["slot_prev"] = slot_uses[q][slot]
                slot_uses[q][slot] += 16
                o["val"] = slot_uses[q][slot]
            elif o["inc"]:
                e = o["eng"]
                ccount[e] += 1
                o["sem"] = csem[e]
                o["val"] = ccount[e]

        def emitter(engname):
            def run(eng):
                water = {}
                for o in self.ops:
                    if o["eng"] != engname:
                        continue
                    waits = {}
                    for d in o["deps"]:
                        p = self.ops[d]
                        if waits.get(p["sem"], -1) < p["val"]:
                            waits[p["sem"]] = p["val"]
                    if o["dma"] and o["slot_prev"] > 0:
                        if waits.get(o["sem"], -1) < o["slot_prev"]:
                            waits[o["sem"]] = o["slot_prev"]
                    for sem, v in waits.items():
                        if water.get(sem.num, -1) >= v:
                            continue
                        eng.wait_ge(sem, v)
                        water[sem.num] = v
                    if o["fn"] is None:
                        continue
                    inst = o["fn"](eng)
                    if o["dma"]:
                        inst.then_inc(o["sem"], 16)
                    elif o["inc"]:
                        inst.then_inc(o["sem"], 1)
            return run

        with nc.Block() as block:
            block.tensor(emitter("tensor"))
            block.vector(emitter("vector"))
            block.scalar(emitter("scalar"))
            block.gpsimd(emitter("gpsimd"))
            block.sync(emitter("sync"))
        return len(self.ops)


def build_nc(upto="full", dbg=False):
    nc = bass.Bass("TRN2", target_bir_lowering=False)
    sch = Sched()
    es = contextlib.ExitStack()

    def din(name, shape, dt=F32):
        return nc.dram_tensor(name, shape, dt, kind="ExternalInput")

    d_x = din("x", [NSEQ, S, DM])
    d_wqkv = din("wqkv", [6, 8, 128, 512])
    d_bqkv_fm = din("bqkv_fm", [128, 24])
    d_bqkv_row = din("bqkv_row", [1, 3 * DM])
    d_wout = din("wout", [8, 128, DM])
    d_bout_rep = din("bout_rep", [128, DM])
    d_winproj = din("winproj", [3, 2, 8, 128, 512])
    d_binproj_fm = din("binproj_fm", [128, 24])
    d_binproj_row = din("binproj_row", [1, 3 * DM])
    d_woutproj = din("woutproj", [8, 128, DM])
    d_boutproj_rep = din("boutproj_rep", [128, DM])
    d_wgate = din("wgate", [128, 64])
    d_bgate = din("bgate", [8, 1])
    d_wexp = din("wexp", [NE, 128, 8 * DM])
    d_bexp = din("bexp", [8, DM])
    d_ln1w = din("ln1w_rep", [128, DM])
    d_ln1b = din("ln1b_rep", [128, DM])
    d_ln2w = din("ln2w_rep", [128, DM])
    d_ln2b = din("ln2b_rep", [128, DM])
    d_identb = din("identb", [128, 128], BF16)
    d_identf = din("identf", [128, 128])
    d_onescol = din("onescol", [128, 1], BF16)
    d_rowconst = din("rowconst", [1, 384], BF16)
    d_trimask = din("trimask", [128, 128], BF16)
    d_ce = din("ce", [S, DM // 2])
    d_se = din("se", [S, DM // 2])

    d_out = nc.dram_tensor("out", [NSEQ, S, DM], F32, kind="ExternalOutput")
    d_dbg = nc.dram_tensor("dbg", [8, 128, 1024], BF16, kind="ExternalOutput") \
        if dbg else None
    d_dbgf = nc.dram_tensor("dbgf", [4, 128, 1024], F32, kind="ExternalOutput") \
        if dbg else None
    d_u = nc.dram_tensor("u_scr", [NSEQ, S, DM], F32)
    d_y1 = nc.dram_tensor("y1_scr", [S, DM], BF16)
    d_moe = nc.dram_tensor("moe_scr", [S, DM], BF16)
    d_mb = nc.dram_tensor("m_scr", [128], F32)

    sb = lambda n, shape, dt: es.enter_context(nc.sbuf_tensor(n, shape, dt))
    psb = lambda n, shape, dt=F32: es.enter_context(nc.psum_tensor(n, shape, dt))

    arX = sb("arX", [128, 16384], BF16)
    arQ = sb("arQ", [128, 16384], BF16)
    arK = sb("arK", [128, 16384], BF16)
    arV = sb("arV", [128, 16384], BF16)
    w1 = sb("w1", [128, 8192], BF16)
    xs = sb("xs", [128, 2048], F32)
    fsb = sb("fsb", [128, 2048], F32)
    tma = sb("tma", [128, 2048], BF16)
    scr = sb("scr", [128, 896], F32)
    sc2 = sb("sc2", [128, 144], F32)
    tbl = sb("tbl", [128, 1024], F32)
    rsc2 = sb("rsc2", [128, 1024], BF16)
    mslot = sb("mslot", [128, 1024], BF16)
    reps = sb("reps", [128, 6 * DM], BF16)
    vrow = sb("vrow", [128, 3072], BF16)
    y1s = sb("y1s", [128, 2048], BF16)
    identb = sb("identb_s", [128, 128], BF16)
    identf = sb("identf_s", [128, 128], F32)
    onescol = sb("onescol_s", [128, 1], BF16)
    trimask = sb("trimask_s", [128, 128], BF16)
    gts = sb("gts", [128, 256], F32)
    rbb = sb("rbb", [128, 256], F32)
    mrow = sb("mrow", [128, 128], BF16)
    wg8 = sb("wg8", [128, 8], BF16)
    etmp = sb("etmp", [128, 1024], BF16)
    geb = sb("geb", [8, DM], BF16)
    gwt = sb("gwt", [128, 64], BF16)
    gbb = sb("gbb", [8, 1], F32)
    bqfm = sb("bqfm", [128, 24], F32)
    epsc = sb("epsc", [128, 1], F32)
    dsink = sb("dsink", [128, 1024], BF16)
    q2b = dsink[:, 0:512]
    bifm = sb("bifm", [128, 24], F32)


    psS = [psb("psS0", [128, 1024]), psb("psS1", [128, 1024])]
    psC = [psb("psC0", [128, 512]), psb("psC1", [128, 512])]
    psL = psb("psL", [128, 512])
    ptr = psb("ptr", [128, 1024], BF16)

    ones_row = vrow[0:1, 2048:2176]
    maskA_row = vrow[0:1, 2176:2304]
    maskB_row = vrow[0:1, 2304:2432]

    def T(fn, rd=(), wr=()):
        sch.add("tensor", fn, rd, wr)

    def V(fn, rd=(), wr=()):
        sch.add("vector", fn, rd, wr)

    def A_(fn, rd=(), wr=()):
        sch.add("scalar", fn, rd, wr)

    def G(fn, rd=(), wr=()):
        sch.add("gpsimd", fn, rd, wr)

    def DMA(queue, fn, rd=(), wr=()):
        sch.add(queue, fn, rd, wr, dma=True)

    def mm(out, lhsT, rhs, start, stop, tp=None):
        return lambda e, o=out, l=lhsT, r=rhs, s=start, st=stop, t=tp: \
            e.matmul(o, l, r, start=s, stop=st, tile_position=t)

    sch.add("vector", lambda e: e.memset(epsc[:, :], EPS), wr=("epsc",))
    for q, src, dst, key in (
        ("sync", d_bqkv_fm[:, :], bqfm[:, :], "bqfm"),
        ("sync", d_binproj_fm[:, :], bifm[:, :], "bifm"),
        ("sync", d_identf[:, :], identf[:, :], "identf"),
        ("sync", d_bgate[:, :], gbb[:, :], "gbb"),
        ("sync", d_identb[:, :], identb[:, :], "identb"),
        ("sync", d_onescol[:, :], onescol[:, :], "onescol"),
        ("sync", d_trimask[:, :], trimask[:, :], "trimask"),
        ("sync", d_rowconst[:, :], vrow[0:1, 2048:2432], "vrowc"),
        ("gpsimd", d_bqkv_row[0:1, 2048:3072], vrow[0:1, 0:1024], "bvA"),
        ("gpsimd", d_binproj_row[0:1, 2048:3072], vrow[0:1, 1024:2048], "bvB"),
        ("gpsimd", d_bexp[:, :], geb[:, :], "geb"),
        ("gpsimd", d_wgate[:, :], gwt[:, :], "gwt"),
        ("gpsimd", d_bout_rep[:, :], reps[:, 0:DM], "reps0"),
        ("gpsimd", d_boutproj_rep[:, :], reps[:, DM:2 * DM], "reps1"),
        ("gpsimd", d_ln1w[:, :], reps[:, 2 * DM:3 * DM], "reps2"),
        ("gpsimd", d_ln1b[:, :], reps[:, 3 * DM:4 * DM], "reps3"),
        ("gpsimd", d_ln2w[:, :], reps[:, 4 * DM:5 * DM], "reps4"),
        ("gpsimd", d_ln2b[:, :], reps[:, 5 * DM:6 * DM], "reps5"),
    ):
        DMA(q, (lambda e, s=src, d=dst: e.dma_start(out=d, in_=s)), rd=(), wr=(key,))

    def ln_stats(src_ap, keys, s):
        b = 24 * s
        A_(lambda e, sp=src_ap: e.activation(dsink[:, :], sp, AF.Identity,
                                             accum_out=sc2[:, b + 21:b + 22]),
           rd=tuple(keys) + ("dsink",), wr=(f"sum{s}", "dsink"))
        A_(lambda e, sp=src_ap: e.activation(dsink[:, :], sp, AF.Square,
                                             accum_out=sc2[:, b + 22:b + 23]),
           rd=tuple(keys) + ("dsink",), wr=(f"sq{s}", "dsink"))
        V(lambda e: e.tensor_scalar(sc2[:, b + 16:b + 17], sc2[:, b + 21:b + 22],
                                    1.0 / 1024.0, None, OP.mult),
          rd=(f"sum{s}",), wr=(f"mean{s}",))
        V(lambda e: e.tensor_mul(sc2[:, b + 18:b + 19], sc2[:, b + 16:b + 17],
                                 sc2[:, b + 16:b + 17]),
          rd=(f"mean{s}",), wr=(f"tmp{s}",))
        V(lambda e: e.scalar_tensor_tensor(sc2[:, b + 17:b + 18],
                                           sc2[:, b + 22:b + 23], 1.0 / 1024.0,
                                           sc2[:, b + 18:b + 19],
                                           OP.mult, OP.subtract),
          rd=(f"sq{s}", f"tmp{s}"), wr=(f"var{s}",))
        A_(lambda e: e.activation(sc2[:, b + 19:b + 20], sc2[:, b + 17:b + 18],
                                  AF.Ln, bias=epsc[:, 0:1]),
           rd=(f"var{s}", "epsc"), wr=(f"rstd{s}",))
        A_(lambda e: e.activation(sc2[:, b + 19:b + 20], sc2[:, b + 19:b + 20],
                                  AF.Exp, scale=-0.5),
           rd=(f"rstd{s}",), wr=(f"rstd{s}",))
        V(lambda e: e.tensor_scalar(sc2[:, b + 20:b + 21], sc2[:, b + 16:b + 17],
                                    sc2[:, b + 19:b + 20], -1.0, OP.mult, OP.mult),
          rd=(f"mean{s}", f"rstd{s}"), wr=(f"nm{s}",))

    def ln_stats_v(src_ap, keys, s):
        b = 24 * s
        V(lambda e, sp=src_ap: e.bn_stats(sc2[:, b:b + 6], sp[:, 0:512]),
          rd=tuple(keys), wr=(f"bn{s}a",))
        V(lambda e, sp=src_ap: e.bn_stats(sc2[:, b + 6:b + 12], sp[:, 512:1024]),
          rd=tuple(keys), wr=(f"bn{s}b",))
        V(lambda e: e.bn_aggr(sc2[:, b + 12:b + 14], sc2[:, b:b + 6]),
          rd=(f"bn{s}a",), wr=(f"mv{s}a",))
        V(lambda e: e.bn_aggr(sc2[:, b + 14:b + 16], sc2[:, b + 6:b + 12]),
          rd=(f"bn{s}b",), wr=(f"mv{s}b",))
        V(lambda e: e.tensor_add(sc2[:, b + 16:b + 17], sc2[:, b + 12:b + 13],
                                 sc2[:, b + 14:b + 15]),
          rd=(f"mv{s}a", f"mv{s}b"), wr=(f"mean{s}",))
        V(lambda e: e.tensor_scalar(sc2[:, b + 16:b + 17], sc2[:, b + 16:b + 17],
                                    0.5, None, OP.mult),
          rd=(f"mean{s}",), wr=(f"mean{s}",))
        V(lambda e: e.tensor_sub(sc2[:, b + 18:b + 19], sc2[:, b + 12:b + 13],
                                 sc2[:, b + 14:b + 15]),
          rd=(f"mv{s}a", f"mv{s}b"), wr=(f"tmp{s}",))
        V(lambda e: e.tensor_mul(sc2[:, b + 18:b + 19], sc2[:, b + 18:b + 19],
                                 sc2[:, b + 18:b + 19]),
          rd=(f"tmp{s}",), wr=(f"tmp{s}",))
        V(lambda e: e.tensor_add(sc2[:, b + 17:b + 18], sc2[:, b + 13:b + 14],
                                 sc2[:, b + 15:b + 16]),
          rd=(f"mv{s}a", f"mv{s}b"), wr=(f"var{s}",))
        V(lambda e: e.tensor_scalar(sc2[:, b + 17:b + 18], sc2[:, b + 17:b + 18],
                                    0.5, None, OP.mult),
          rd=(f"var{s}",), wr=(f"var{s}",))
        V(lambda e: e.scalar_tensor_tensor(sc2[:, b + 17:b + 18],
                                           sc2[:, b + 18:b + 19], 0.25,
                                           sc2[:, b + 17:b + 18], OP.mult, OP.add),
          rd=(f"tmp{s}", f"var{s}"), wr=(f"var{s}",))
        A_(lambda e: e.activation(sc2[:, b + 19:b + 20], sc2[:, b + 17:b + 18],
                                  AF.Ln, bias=epsc[:, 0:1]),
           rd=(f"var{s}", "epsc"), wr=(f"rstd{s}",))
        A_(lambda e: e.activation(sc2[:, b + 19:b + 20], sc2[:, b + 19:b + 20],
                                  AF.Exp, scale=-0.5),
           rd=(f"rstd{s}",), wr=(f"rstd{s}",))
        V(lambda e: e.tensor_scalar(sc2[:, b + 20:b + 21], sc2[:, b + 16:b + 17],
                                    sc2[:, b + 19:b + 20], -1.0, OP.mult, OP.mult),
          rd=(f"mean{s}", f"rstd{s}"), wr=(f"nm{s}",))

    def ln_apply(dst_ap, src_ap, rdk, wrk, s, eng="scalar"):
        b = 24 * s
        if eng == "scalar":
            A_(lambda e, d=dst_ap, sp=src_ap: e.activation(
                d, sp, AF.Identity, bias=sc2[:, b + 20:b + 21],
                scale=sc2[:, b + 19:b + 20]),
               rd=tuple(rdk) + (f"rstd{s}", f"nm{s}"), wr=tuple(wrk))
        else:
            V(lambda e, d=dst_ap, sp=src_ap: e.tensor_scalar(
                d, sp, sc2[:, b + 19:b + 20], sc2[:, b + 20:b + 21],
                OP.mult, OP.add),
              rd=tuple(rdk) + (f"rstd{s}", f"nm{s}"), wr=tuple(wrk))

    def tm_to_fm(tmslot_ap, fm_h, t, rdk, prefix):
        for dc in range(8):
            T(lambda e, sp=tmslot_ap, d=dc: e.transpose(
                ptr[:, d * 128:(d + 1) * 128], sp[:, d * 128:(d + 1) * 128],
                identb[:, :]),
              rd=tuple(rdk) + ("identb",), wr=(f"ptr{dc}",))
        out_ap = fm_h[:, :].rearrange("p (c q) -> p c q", c=8)[:, :, t * 128:(t + 1) * 128]
        in_ap = ptr[:, :].rearrange("p (c q) -> p c q", c=8)
        V(lambda e, o=out_ap, i=in_ap: e.tensor_copy(o, i),
          rd=tuple(f"ptr{dc}" for dc in range(8)),
          wr=tuple(f"{prefix}_{dc}_{t}" for dc in range(8)))

    def qk_keys(pref):
        return tuple(f"{pref}_{oc}_tc{tc}" for oc in range(8) for tc in range(4))

    def fm_keys(prefix, dc=None, ts=None):
        dcs = range(8) if dc is None else [dc]
        tts = range(NT) if ts is None else ts
        return tuple(f"{prefix}_{d}_{t}" for d in dcs for t in tts)


    def dump(idx, ap, keys, f32=False, w=1024):
        dst = (d_dbgf if f32 else d_dbg)
        DMA("sync", lambda e, i=idx, a=ap, ww=w: e.dma_start(
            out=dst[i, :, 0:ww], in_=a), rd=tuple(keys),
            wr=(f"outd_dbg{'f' if f32 else ''}{idx}",))

    def emit_seq(si, tail_cb=None):
        sq = f"s{si}"

        for t in range(NT):
            xslot = xs[:, (t % 2) * 1024:(t % 2) * 1024 + 1024]
            xk = f"xs{t % 2}"
            DMA("sync", lambda e, tt=t, d=xslot: e.dma_start(
                out=d, in_=d_x[si, tt * 128:(tt + 1) * 128, :]), rd=(), wr=(xk,))
            st = t % 2
            ln_stats(xslot, (xk,), st)
            tslot = tma[:, st * 1024:st * 1024 + 1024]
            tk = f"tma{st}"
            ln_apply(tslot, xslot, (xk,), (tk,), st)
            tm_to_fm(tslot, arX, t, (tk,), "h0")

        def emit_qkv(src_h, src_pref, wq_dram, bias_sb, dst_list, vrow_off,
                     cb=None):
            for og in range(4):
                wk = f"w1_{og % 2}"
                for dc in range(8):
                    DMA("gpsimd", lambda e, d=dc, o=og: e.dma_start(
                        out=w1[:, (o % 2) * 4096 + d * 512:
                               (o % 2) * 4096 + (d + 1) * 512],
                        in_=wq_dram(o, d)),
                        rd=(), wr=(wk,))
                dsth, pref = dst_list[og // 2]
                for tc in range(4):
                    for oc4 in range(4):
                        oc = (og % 2) * 4 + oc4
                        un, hh = oc4 // 2, oc4 % 2
                        for dc in range(8):
                            T(mm(psS[un][:, hh * 512:hh * 512 + 512],
                                 w1[:, (og % 2) * 4096 + dc * 512 + oc4 * 128:
                                    (og % 2) * 4096 + dc * 512 + oc4 * 128 + 128],
                                 src_h[:, dc * S + tc * 512: dc * S + tc * 512 + 512],
                                 dc == 0, dc == 7),
                              rd=(wk,) + fm_keys(src_pref, dc, range(tc * 4, tc * 4 + 4)),
                              wr=(f"psS{un}_{hh}",))
                        A_(lambda e, dh=dsth, o=oc, c=tc, u=un, h=hh,
                           b=bias_sb[:, (og // 2) * 8 + oc:(og // 2) * 8 + oc + 1]:
                           e.activation(dh[:, o * S + c * 512: o * S + c * 512 + 512],
                                        psS[u][:, h * 512:h * 512 + 512],
                                        AF.Identity, bias=b),
                           rd=(f"psS{un}_{hh}", "biasfm"),
                           wr=(f"{pref}_{oc}_tc{tc}",))
                    if cb is not None:
                        cb(og * 4 + tc)
            for vog in range(2):
                wk = f"w1_{vog}"
                for dc in range(8):
                    DMA("gpsimd", lambda e, d=dc, o=vog: e.dma_start(
                        out=w1[:, o * 4096 + d * 512: o * 4096 + (d + 1) * 512],
                        in_=wq_dram(4 + o, d)),
                        rd=(), wr=(wk,))
                for t in range(NT):
                    pc = psC[t % 2]
                    T(mm(pc[:, :], ones_row,
                         vrow[0:1, vrow_off + vog * 512: vrow_off + vog * 512 + 512],
                         True, False),
                      rd=("vrowc", "bvrow"), wr=(f"psC{t % 2}",))
                    for dc in range(8):
                        T(mm(pc[:, :], src_h[:, dc * S + t * 128: dc * S + t * 128 + 128],
                             w1[:, vog * 4096 + dc * 512: vog * 4096 + (dc + 1) * 512],
                             False, dc == 7),
                          rd=(wk,) + fm_keys(src_pref, dc, [t]), wr=(f"psC{t % 2}",))
                    V(lambda e, tt=t, o=vog, p=pc: e.tensor_copy(
                        arV[:, tt * DM + o * 512: tt * DM + o * 512 + 512], p[:, :]),
                      rd=(f"psC{t % 2}",), wr=(f"v_{t}_{vog}",))

        sch.fence(("bqfm", "bvA"), ("biasfm", "bvrow"))
        emit_qkv(arX, "h0", lambda s_, d: d_wqkv[s_, d], bqfm,
                 [(arQ, "q"), (arK, "k")], 0,
                 cb=(tail_cb[0] if tail_cb is not None else None))

        sch.fence(("w1_0", "w1_1"), tuple(f"w1res{d}" for d in range(8)))
        for dc in range(8):
            DMA("gpsimd", lambda e, d=dc: e.dma_start(
                out=w1[:, d * DM:(d + 1) * DM], in_=d_wout[d]),
                rd=(), wr=(f"w1res{dc}",))

        def emit_norms_A():
            for which, pref, src in (("q", "q", arQ), ("k", "k", arK)):
                for tc in range(4):
                    for dc in range(8):
                        V(lambda e, s_=src, d=dc, c=tc: e.tensor_mul(
                            q2b, s_[:, d * S + c * 512: d * S + c * 512 + 512],
                            s_[:, d * S + c * 512: d * S + c * 512 + 512]),
                          rd=(f"{pref}_{dc}_tc{tc}",), wr=("q2b", "dsink"))
                        T(mm(psL[0:1, 0:512], onescol[:, 0:1], q2b,
                             dc == 0, dc == 7),
                          rd=("q2b", "dsink", "onescol"), wr=("psL",))
                    if which == "q":
                        for h in range(2):
                            V(lambda e, c=tc, hh=h: e.reduce_max(
                                sc2[0:1, 102 + 2 * c + hh: 103 + 2 * c + hh],
                                psL[0:1, hh * 256:hh * 256 + 256], axis=AX),
                              rd=("psL",), wr=(f"qmaxA{2 * tc + h}",))
                    else:
                        V(lambda e, c=tc: e.reduce_max(
                            sc2[0:1, 110 + c:111 + c], psL[0:1, 0:512], axis=AX),
                          rd=("psL",), wr=(f"kmaxAt{tc}",))
            V(lambda e: e.reduce_max(sc2[0:1, 114:115], sc2[0:1, 110:114], axis=AX),
              rd=tuple(f"kmaxAt{c}" for c in range(4)), wr=("kmaxA",))
            V(lambda e: e.tensor_scalar(sc2[0:1, 115:123], sc2[0:1, 102:110],
                                        sc2[0:1, 114:115], SCA * SCA, OP.mult, OP.mult),
              rd=tuple(f"qmaxA{j}" for j in range(8)) + ("kmaxA",), wr=("mtA",))
            A_(lambda e: e.activation(sc2[0:1, 115:123], sc2[0:1, 115:123], AF.Ln),
               rd=("mtA",), wr=("mtA",))
            A_(lambda e: e.activation(sc2[0:1, 115:123], sc2[0:1, 115:123], AF.Exp,
                                      scale=0.5), rd=("mtA",), wr=("mtA",))
            V(lambda e: e.tensor_scalar(mrow[0:1, 0:8], sc2[0:1, 115:123], -1.0,
                                        None, OP.mult), rd=("mtA",), wr=("negmA",))
            for j in range(8):
                T(mm(psL[:, j:j + 1], ones_row, mrow[0:1, j:j + 1], True, True),
                  rd=("negmA", "vrowc"), wr=("psL",))
            V(lambda e: e.tensor_copy(scr[:, 4:12], psL[:, 0:8]),
              rd=("psL",), wr=("biasA",))

        if dbg and si == 0:
            dump(0, arX[:, 0:1024], fm_keys("h0", 0, range(8)))
            dump(1, arQ[:, 0:1024], ("q_0_tc0", "q_0_tc1"))
            dump(2, arK[:, 0:1024], ("k_0_tc0", "k_0_tc1"))
            dump(3, arV[:, 0:1024], ("v_0_0", "v_0_1"))

        emit_norms_A()
        if dbg and si == 0:
            dump(0, scr[:, 0:1024], ("biasA",), f32=True)
            DMA("sync", lambda e: e.dma_start(out=d_dbgf[3, :, 0:96],
                                              in_=sc2[:, 0:96]),
                rd=("biasA", "mtA"), wr=("outd_dbgsc2",))

        sch.fence(fm_keys("h0"), tuple(f"pt{k}" for k in range(NT)) +
                  tuple(f"cx{d}" for d in range(8)))
        for qb in range(NQB):
            for ku in range(4):
                un = ku % 2
                for ki in range(4):
                    kt = ku * 4 + ki
                    for dc in range(8):
                        T(mm(psS[un][:, ki * 256:ki * 256 + 256],
                             arK[:, dc * S + kt * 128: dc * S + kt * 128 + 128],
                             arQ[:, dc * S + qb * 256: dc * S + qb * 256 + 256],
                             dc == 0, dc == 7),
                          rd=(f"k_{dc}_tc{kt // 4}", f"q_{dc}_tc{qb // 2}"),
                          wr=(f"psS{un}_{ki // 2}",))
                A_(lambda e, u=un, k=ku, q=qb: e.activation(
                    arX[:, k * 1024: k * 1024 + 1024], psS[u][:, :], AF.Exp,
                    bias=scr[:, 4 + q:5 + q], scale=SCA),
                   rd=(f"psS{un}_0", f"psS{un}_1", "biasA"),
                   wr=tuple(f"pt{4 * ku + i}" for i in range(4)))
            if dbg and si == 0 and qb == 0:
                dump(4, arX[:, 0:1024], tuple(f"pt{k}" for k in range(4)))
            for kt in range(NT):
                T(mm(psL[0:1, 0:256], onescol[:, 0:1],
                     arX[:, kt * 256: kt * 256 + 256], kt == 0, kt == NT - 1),
                  rd=(f"pt{kt}", "onescol"), wr=("psL",))
            V(lambda e: e.tensor_copy(scr[0:1, 96:352], psL[0:1, 0:256]),
              rd=("psL",), wr=("lrow",))
            for dc in range(8):
                pc = psC[dc % 2]
                for kt in range(NT):
                    T(mm(pc[:, 0:256],
                         arV[:, kt * DM + dc * 128: kt * DM + dc * 128 + 128],
                         arX[:, kt * 256: kt * 256 + 256], kt == 0, kt == NT - 1),
                      rd=(f"v_{kt}_{dc // 4}", f"pt{kt}"), wr=(f"psC{dc % 2}",))
                V(lambda e, d=dc, p=pc: e.tensor_copy(
                    arX[:, 8192 + d * 256: 8192 + (d + 1) * 256], p[:, 0:256]),
                  rd=(f"psC{dc % 2}",), wr=(f"cx{dc}",))
            for j in range(2):
                T(lambda e, jj=j: e.transpose(
                    psL[:, jj:jj + 1], scr[0:1, 96 + jj * 128: 96 + (jj + 1) * 128],
                    identf[0:1, 0:1]),
                  rd=("lrow", "identf"), wr=("psL",))
            V(lambda e, q=qb: e.reciprocal(scr[:, 76 + q * 2: 78 + q * 2],
                                           psL[:, 0:2]),
              rd=("psL",), wr=(f"rtm{qb}",))
            if dbg and si == 0 and qb == 0:
                dump(5, arX[:, 8192:9216], tuple(f"cx{d}" for d in range(4)))
                dump(1, scr[:, 0:1024], (f"rtm{qb}", "lrow"), f32=True)
            for j in range(2):
                t = qb * 2 + j
                for dc in range(8):
                    for vc in range(2):
                        T(mm(psS[0][:, vc * 512:vc * 512 + 512],
                             arX[:, 8192 + dc * 256 + j * 128:
                                 8192 + dc * 256 + j * 128 + 128],
                             w1[:, dc * DM + vc * 512: dc * DM + vc * 512 + 512],
                             dc == 0, dc == 7),
                          rd=(f"cx{dc}", f"w1res{dc}"), wr=(f"psS0_{vc}",))
                aslot = fsb[:, (j % 2) * 1024:(j % 2) * 1024 + 1024]
                ak = f"fs{j % 2}"
                V(lambda e, a=aslot, q=qb, jj=j: e.scalar_tensor_tensor(
                    a, psS[0][:, :], scr[:, 76 + q * 2 + jj: 77 + q * 2 + jj],
                    reps[:, 0:DM], OP.mult, OP.add),
                  rd=("psS0_0", "psS0_1", f"rtm{qb}", "reps0"), wr=(ak,))
                if dbg and si == 0 and qb == 0 and j == 0:
                    dump(2, aslot, (ak,), f32=True)
                    dump(7, w1[:, 0:1024], ("w1res0",))
                st = j % 2
                ln_stats_v(aslot, (ak,), st)
                ln_apply(aslot, aslot, (ak,), (ak,), st)
                xslot = xs[:, st * 1024:st * 1024 + 1024]
                xk = f"xs{st}"
                DMA("sync", lambda e, tt=t, d=xslot: e.dma_start(
                    out=d, in_=d_x[si, tt * 128:(tt + 1) * 128, :]), rd=(), wr=(xk,))
                V(lambda e, a=aslot, x_=xslot: e.tensor_add(a, a, x_),
                  rd=(ak, xk), wr=(ak,))
                DMA("sync", lambda e, tt=t, a=aslot: e.dma_start(
                    out=d_u[si, tt * 128:(tt + 1) * 128, :], in_=a),
                    rd=(ak,), wr=(f"u_{t}",))

        if upto == "A":
            if dbg and si == 1:
                dump(6, w1[:, 0:1024], ("w1res0",))
            for t in range(NT):
                DMA("sync", lambda e, tt=t: e.dma_start(
                    out=xs[:, 0:1024], in_=d_u[si, tt * 128:(tt + 1) * 128, :]),
                    rd=(f"u_{t}",), wr=("xs0",))
                DMA("sync", lambda e, tt=t: e.dma_start(
                    out=d_out[si, tt * 128:(tt + 1) * 128, :], in_=xs[:, 0:1024]),
                    rd=("xs0",), wr=(f"outd_{si}_{t}",))
            sch.fence((), fm_keys("h0") + qk_keys("q") + qk_keys("k") +
                      tuple(f"v_{t}_{o}" for t in range(NT) for o in range(2)) +
                      ("w1_0", "w1_1") +
                      tuple(f"pt{k}" for k in range(NT)) +
                      tuple(f"cx{d}" for d in range(8)))
            return None

        sch.fence(tuple(f"pt{k}" for k in range(NT)) +
                  tuple(f"cx{d}" for d in range(8)), fm_keys("g"))
        sch.fence(tuple(f"v_{t}_{o}" for t in range(NT) for o in range(2)) +
                  qk_keys("q") + qk_keys("k"),
                  ("arVnew", "arQnew", "arKnew"))
        for t in range(NT):
            xslot = xs[:, (t % 2) * 1024:(t % 2) * 1024 + 1024]
            xk = f"xs{t % 2}"
            DMA("sync", lambda e, tt=t, d=xslot: e.dma_start(
                out=d, in_=d_u[si, tt * 128:(tt + 1) * 128, :]),
                rd=(f"u_{t}",), wr=(xk,))
            st = t % 2
            ln_stats(xslot, (xk,), st)
            tslot = tma[:, st * 1024:st * 1024 + 1024]
            tk = f"tma{st}"
            ln_apply(tslot, xslot, (xk,), (tk,), st)
            tm_to_fm(tslot, arX, t, (tk,), "g")
        sch.fence(("bifm", "bvB", "arVnew", "arQnew", "arKnew"),
                  ("biasfm", "bvrow"))
        sch.fence(tuple(f"w1res{d}" for d in range(8)), ("w1_0", "w1_1"))
        for vog in range(2):
            wk = f"w1_{vog}"
            for dc in range(8):
                DMA("gpsimd", lambda e, d=dc, o=vog: e.dma_start(
                    out=w1[:, o * 4096 + d * 512: o * 4096 + (d + 1) * 512],
                    in_=d_winproj[2, o, d]),
                    rd=(), wr=(wk,))
            for t in range(NT):
                pc = psC[t % 2]
                T(mm(pc[:, :], ones_row,
                     vrow[0:1, 1024 + vog * 512: 1024 + vog * 512 + 512],
                     True, False),
                  rd=("vrowc", "bvrow"), wr=(f"psC{t % 2}",))
                for dc in range(8):
                    T(mm(pc[:, :], arX[:, dc * S + t * 128: dc * S + t * 128 + 128],
                         w1[:, vog * 4096 + dc * 512: vog * 4096 + (dc + 1) * 512],
                         False, dc == 7),
                      rd=(wk,) + fm_keys("g", dc, [t]), wr=(f"psC{t % 2}",))
                V(lambda e, tt=t, o=vog, p=pc: e.tensor_copy(
                    arV[:, tt * DM + o * 512: tt * DM + o * 512 + 512], p[:, :]),
                  rd=(f"psC{t % 2}",), wr=(f"v_{t}_{vog}",))

        sch.fence(fm_keys("g"), fm_keys("r"))
        for t in range(NT):
            xslot = xs[:, (t % 2) * 1024:(t % 2) * 1024 + 1024]
            xk = f"xs{t % 2}"
            DMA("sync", lambda e, tt=t, d=xslot: e.dma_start(
                out=d, in_=d_u[si, tt * 128:(tt + 1) * 128, :]),
                rd=(f"u_{t}",), wr=(xk,))
            st = t % 2
            ln_stats(xslot, (xk,), st)
            gslot = tma[:, 0:1024]
            ln_apply(gslot, xslot, (xk,), ("tma0",), st)
            ce = tbl[:, 0:512]
            se = tbl[:, 512:1024]
            DMA("sync", lambda e, tt=t, d=ce: e.dma_start(
                out=d, in_=d_ce[tt * 128:(tt + 1) * 128, :]), rd=("tbl0",), wr=("tbl0w",))
            DMA("sync", lambda e, tt=t, d=se: e.dma_start(
                out=d, in_=d_se[tt * 128:(tt + 1) * 128, :]), rd=("tbl1",), wr=("tbl1w",))
            rslot = tma[:, 1024:2048]
            t1 = rsc2[:, 0:512]
            t2 = rsc2[:, 512:1024]
            V(lambda e, g=gslot, c=ce: e.tensor_mul(t1, g[:, 0:1024:2], c),
              rd=("tma0", "tbl0w"), wr=("rt1",))
            V(lambda e, g=gslot, s_=se: e.tensor_mul(t2, g[:, 1:1024:2], s_),
              rd=("tma0", "tbl1w"), wr=("rt2",))
            V(lambda e, r=rslot: e.tensor_sub(r[:, 0:1024:2], t1, t2),
              rd=("rt1", "rt2"), wr=("tma1e",))
            V(lambda e, g=gslot, s_=se: e.tensor_mul(t1, g[:, 0:1024:2], s_),
              rd=("tma0", "tbl1w", "tma1e"), wr=("rt1",))
            V(lambda e, g=gslot, c=ce: e.tensor_mul(t2, g[:, 1:1024:2], c),
              rd=("tma0", "tbl0w", "tma1e"), wr=("rt2",))
            V(lambda e, r=rslot: e.tensor_add(r[:, 1:1024:2], t1, t2),
              rd=("rt1", "rt2"), wr=("tma1o",))
            sch.fence(("tbl0w",), ("tbl0",))
            sch.fence(("tbl1w",), ("tbl1",))
            tm_to_fm(rslot, arX, t, ("tma1e", "tma1o"), "r")
        for part, dsth, pref in ((0, arQ, "q"), (1, arK, "k")):
            for og in range(2):
                wk = f"w1_{og}"
                for dc in range(8):
                    DMA("gpsimd", lambda e, p=part, d=dc, o=og: e.dma_start(
                        out=w1[:, o * 4096 + d * 512: o * 4096 + (d + 1) * 512],
                        in_=d_winproj[p, o, d]),
                        rd=(), wr=(wk,))
                for tc in range(4):
                    for oc4 in range(4):
                        oc = og * 4 + oc4
                        un, hh = oc4 // 2, oc4 % 2
                        for dc in range(8):
                            T(mm(psS[un][:, hh * 512:hh * 512 + 512],
                                 w1[:, og * 4096 + dc * 512 + oc4 * 128:
                                    og * 4096 + dc * 512 + oc4 * 128 + 128],
                                 arX[:, dc * S + tc * 512: dc * S + tc * 512 + 512],
                                 dc == 0, dc == 7),
                              rd=(wk,) + fm_keys("r", dc, range(tc * 4, tc * 4 + 4)),
                              wr=(f"psS{un}_{hh}",))
                        A_(lambda e, dh=dsth, o=oc, c=tc, u=un, h=hh,
                           b=bifm[:, part * 8 + oc:part * 8 + oc + 1]:
                           e.activation(dh[:, o * S + c * 512: o * S + c * 512 + 512],
                                        psS[u][:, h * 512:h * 512 + 512],
                                        AF.Identity, bias=b),
                           rd=(f"psS{un}_{hh}", "biasfm"),
                           wr=(f"{pref}_{oc}_tc{tc}",))

        def emit_norms_B():
            for which, pref, src, bcol in (("q", "q", arQ, 608), ("k", "k", arK, 672)):
                for pc in range(8):
                    for tc in range(4):
                        V(lambda e, s_=src, p=pc, c=tc: e.tensor_mul(
                            q2b, s_[:, p * S + c * 512: p * S + c * 512 + 512],
                            s_[:, p * S + c * 512: p * S + c * 512 + 512]),
                          rd=(f"{pref}_{pc}_tc{tc}",), wr=("q2b", "dsink"))
                        T(mm(psL[0:1, 0:512], onescol[:, 0:1], q2b,
                             True, True),
                          rd=("q2b", "dsink", "onescol"), wr=("psL",))
                        if which == "q":
                            for h in range(2):
                                V(lambda e, p=pc, c=tc, hh=h: e.reduce_max(
                                    scr[0:1, 608 + p * 8 + c * 2 + hh:
                                        609 + p * 8 + c * 2 + hh],
                                    psL[0:1, hh * 256:hh * 256 + 256], axis=AX),
                                  rd=("psL",), wr=("nbq",))
                        else:
                            V(lambda e, p=pc, c=tc: e.reduce_max(
                                scr[0:1, 672 + p * 4 + c: 673 + p * 4 + c],
                                psL[0:1, 0:512], axis=AX),
                              rd=("psL",), wr=("nbk",))
            DMA("sync", lambda e: e.dma_start(out=d_mb[0:96],
                                              in_=scr[0:1, 608:704]),
                rd=("nbq", "nbk"), wr=("mbd",))
            DMA("sync", lambda e: e.dma_start(
                out=sc2[0:8, 123:131], in_=d_mb[0:64].rearrange("(p c) -> p c", p=8)),
                rd=("mbd",), wr=("qnB",))
            DMA("sync", lambda e: e.dma_start(
                out=sc2[0:8, 131:135], in_=d_mb[64:96].rearrange("(p c) -> p c", p=8)),
                rd=("mbd",), wr=("knB",))
            V(lambda e: e.reduce_max(sc2[0:8, 135:136], sc2[0:8, 131:135], axis=AX),
              rd=("knB",), wr=("knBm",))
            V(lambda e: e.tensor_scalar(sc2[0:8, 136:144], sc2[0:8, 123:131],
                                        sc2[0:8, 135:136], SCB * SCB, OP.mult, OP.mult),
              rd=("qnB", "knBm"), wr=("mtB",))
            A_(lambda e: e.activation(sc2[0:8, 136:144], sc2[0:8, 136:144], AF.Ln),
               rd=("mtB",), wr=("mtB",))
            A_(lambda e: e.activation(sc2[0:8, 136:144], sc2[0:8, 136:144], AF.Exp,
                                      scale=0.5), rd=("mtB",), wr=("mtB",))
            V(lambda e: e.tensor_scalar(sc2[0:8, 136:144], sc2[0:8, 136:144], -1.0,
                                        None, OP.mult), rd=("mtB",), wr=("mtB",))
            DMA("sync", lambda e: e.dma_start(out=d_mb[0:64], in_=sc2[0:8, 136:144]),
                rd=("mtB",), wr=("mbd",))
            DMA("sync", lambda e: e.dma_start(out=scr[0:1, 608:672],
                                              in_=d_mb[0:64]),
                rd=("mbd",), wr=("negmB",))
            V(lambda e: e.tensor_copy(mrow[0:1, 0:64], scr[0:1, 608:672]),
              rd=("negmB",), wr=("negmBb",))
            for j in range(64):
                T(mm(psL[:, j:j + 1], ones_row, mrow[0:1, j:j + 1], True, True),
                  rd=("negmBb", "vrowc"), wr=("psL",))
            V(lambda e: e.tensor_copy(scr[:, 12:76], psL[:, 0:64]),
              rd=("psL",), wr=("biasB",))

        emit_norms_B()

        sch.fence(("w1_0", "w1_1"), tuple(f"w1res{d}" for d in range(8)))
        for dc in range(8):
            DMA("gpsimd", lambda e, d=dc: e.dma_start(
                out=w1[:, d * DM:(d + 1) * DM], in_=d_woutproj[d]),
                rd=(), wr=(f"w1res{dc}",))

        if upto == "bqkv":
            for t in range(NT):
                DMA("sync", lambda e, tt=t: e.dma_start(
                    out=xs[:, 0:1024], in_=d_u[si, tt * 128:(tt + 1) * 128, :]),
                    rd=(f"u_{t}",), wr=("xs0",))
                DMA("sync", lambda e, tt=t: e.dma_start(
                    out=d_out[si, tt * 128:(tt + 1) * 128, :], in_=xs[:, 0:1024]),
                    rd=("xs0",), wr=(f"outd_{si}_{t}",))
            sch.fence((), tuple(f"pt{k}" for k in range(NT)) +
                      tuple(f"cx{d}" for d in range(8)) +
                      fm_keys("h0") + fm_keys("g") + fm_keys("r") +
                      qk_keys("q") + qk_keys("k") +
                      tuple(f"v_{t}_{o}" for t in range(NT) for o in range(2)) +
                      ("w1_0", "w1_1") + tuple(f"w1res{d}" for d in range(8)))
            return None

        sch.fence(fm_keys("r"), tuple(f"pt{k}" for k in range(NT)) +
                  tuple(f"cx{d}" for d in range(8)))
        for qb in range(BATTN["nqb"]):
            nkt = 2 * qb + 2
            for pc in range(8):
                for ku in range(nkt // 2):
                    un = ku % 2
                    for ki in range(2):
                        kt = ku * 2 + ki
                        n0 = 128 if kt == 2 * qb + 1 else 0
                        for hh in range(2):
                            T(mm(psS[un][:, hh * 512 + ki * 256 + n0:
                                         hh * 512 + ki * 256 + 256],
                                 arK[hh * 64:hh * 64 + 64,
                                     pc * S + kt * 128: pc * S + kt * 128 + 128],
                                 arQ[hh * 64:hh * 64 + 64,
                                     pc * S + qb * 256 + n0: pc * S + qb * 256 + 256],
                                 True, True, tp=(hh * 64, 0)),
                              rd=(f"k_{pc}_tc{kt // 4}", f"q_{pc}_tc{qb // 2}"),
                              wr=(f"psS{un}_{hh}",))
                    in3 = psS[un][:, :].rearrange("p (h q) -> p h q", h=2)
                    out3 = arX[:, :].rearrange("p (h z) -> p h z", h=4)[
                        :, 0:2, ku * 512: ku * 512 + 512]
                    A_(lambda e, i3=in3, o3=out3, p=pc, q=qb: e.activation(
                        o3, i3, AF.Exp,
                        bias=scr[:, 12 + p * 8 + q: 13 + p * 8 + q], scale=SCB),
                       rd=(f"psS{un}_0", f"psS{un}_1", "biasB"),
                       wr=(f"pt{2 * ku}", f"pt{2 * ku + 1}"))
                ktd = 2 * qb
                for hh in range(2) if BATTN["mask"] else []:
                    G(lambda e, k=ktd, h2=hh: e.tensor_mul(
                        arX[:, h2 * 4096 + k * 256: h2 * 4096 + k * 256 + 128],
                        arX[:, h2 * 4096 + k * 256: h2 * 4096 + k * 256 + 128],
                        trimask[:, :]),
                      rd=(f"pt{ktd}", "trimask"), wr=(f"pt{ktd}",))
                    G(lambda e, k=ktd + 1, h2=hh: e.memset(
                        arX[:, h2 * 4096 + k * 256: h2 * 4096 + k * 256 + 128], 0.0),
                      rd=(), wr=(f"pt{ktd + 1}",))
                    G(lambda e, k=ktd + 1, h2=hh: e.tensor_mul(
                        arX[:, h2 * 4096 + k * 256 + 128:
                            h2 * 4096 + k * 256 + 256],
                        arX[:, h2 * 4096 + k * 256 + 128:
                            h2 * 4096 + k * 256 + 256],
                        trimask[:, :]),
                      rd=(f"pt{ktd + 1}", "trimask"), wr=(f"pt{ktd + 1}",))
                for kt in range(nkt) if BATTN["lsum"] else []:
                    T(mm(psL[0:1, 0:256], onescol[:, 0:1],
                         arX[:, kt * 256: kt * 256 + 256], kt == 0, kt == nkt - 1),
                      rd=(f"pt{kt}", "onescol"), wr=("psL",))
                    T(mm(psL[0:1, 256:512], onescol[:, 0:1],
                         arX[:, 4096 + kt * 256: 4096 + kt * 256 + 256],
                         False, kt == nkt - 1),
                      rd=(f"pt{kt}", "onescol"), wr=("psL",))
                if not BATTN["lsum"]:
                    continue
                cp = psC[pc % 2]
                for kt in range(nkt) if BATTN["ctx"] else []:
                    for hh in range(2):
                        T(mm(cp[hh * 64:hh * 64 + 64, 0:256],
                             arV[:, kt * DM + (2 * pc + hh) * 64:
                                 kt * DM + (2 * pc + hh) * 64 + 64],
                             arX[:, hh * 4096 + kt * 256:
                                 hh * 4096 + kt * 256 + 256],
                             kt == 0, kt == nkt - 1, tp=(0, hh * 64)),
                          rd=(f"v_{kt}_{(2 * pc + hh) // 8}", f"pt{kt}"),
                          wr=(f"psC{pc % 2}",))
                V(lambda e: e.tensor_copy(vrow[0:1, 2432:2944], psL[0:1, 0:512]),
                  rd=("psL",), wr=("rrowb",))
                T(mm(psL[:, 0:256], maskA_row, vrow[0:1, 2432:2688], True, False),
                  rd=("rrowb", "vrowc"), wr=("psL",))
                T(mm(psL[:, 0:256], maskB_row, vrow[0:1, 2688:2944], False, True),
                  rd=("rrowb", "vrowc"), wr=("psL",))
                V(lambda e: e.reciprocal(rbb[:, :], psL[:, 0:256]),
                  rd=("psL",), wr=("rbb",))
                if BATTN["ctx"]:
                    V(lambda e, p=pc, c=cp: e.tensor_mul(
                        arX[:, 8192 + p * 256: 8192 + (p + 1) * 256], c[:, 0:256],
                        rbb[:, :]),
                      rd=(f"psC{pc % 2}", "rbb"), wr=(f"cx{pc}",))
            if BATTN["epi"]:
                for j in range(2):
                    t = qb * 2 + j
                    st = j % 2
                    xslot = xs[:, st * 1024:st * 1024 + 1024]
                    xk = f"xs{st}"
                    DMA("sync", lambda e, tt=t, d=xslot: e.dma_start(
                        out=d, in_=d_u[si, tt * 128:(tt + 1) * 128, :]),
                        rd=(f"u_{t}",), wr=(xk,))
                    ln_stats_v(xslot, (xk,), st)
                    gslot = fsb[:, st * 1024:st * 1024 + 1024]
                    ln_apply(gslot, xslot, (xk,), (f"fs{st}",), st)
            for j in range(2) if BATTN["epi"] else []:
                t = qb * 2 + j
                un = j
                for dc in range(8):
                    for vc in range(2):
                        T(mm(psS[un][:, vc * 512:vc * 512 + 512],
                             arX[:, 8192 + dc * 256 + j * 128:
                                 8192 + dc * 256 + j * 128 + 128],
                             w1[:, dc * DM + vc * 512: dc * DM + vc * 512 + 512],
                             dc == 0, dc == 7),
                          rd=(f"cx{dc}", f"w1res{dc}"), wr=(f"psS{un}_{vc}",))
                st = j % 2
                gslot = fsb[:, st * 1024:st * 1024 + 1024]
                gk = f"fs{st}"
                V(lambda e, g=gslot, u=un: e.scalar_tensor_tensor(
                    g, psS[u][:, :], 1.0, g, OP.mult, OP.add),
                  rd=(f"psS{un}_0", f"psS{un}_1", gk), wr=(gk,))
                V(lambda e, g=gslot: e.tensor_add(g, g, reps[:, DM:2 * DM]),
                  rd=(gk, "reps1"), wr=(gk,))
                ln_stats_v(gslot, (gk,), st)
                ln_apply(gslot, gslot, (gk,), (gk,), st)
                V(lambda e, g=gslot: e.tensor_mul(g, g, reps[:, 2 * DM:3 * DM]),
                  rd=(gk, "reps2"), wr=(gk,))
                yslot = y1s[:, st * 1024:st * 1024 + 1024]
                yk = f"y1s{st}"
                V(lambda e, y=yslot, g=gslot: e.tensor_add(
                    y, g, reps[:, 3 * DM:4 * DM]),
                  rd=(gk, "reps3"), wr=(yk,))
                DMA("sync", lambda e, tt=t, y=yslot: e.dma_start(
                    out=d_y1[tt * 128:(tt + 1) * 128, :], in_=y),
                    rd=(yk,), wr=(f"y1d_{t}",))

        if upto == "battn":
            for t in range(NT):
                DMA("gpsimd", lambda e, tt=t: e.dma_start(
                    out=xs[:, 0:1024], in_=d_y1[tt * 128:(tt + 1) * 128, :]),
                    rd=(f"y1d_{t}",), wr=("xs0",))
                DMA("sync", lambda e, tt=t: e.dma_start(
                    out=d_out[si, tt * 128:(tt + 1) * 128, :], in_=xs[:, 0:1024]),
                    rd=("xs0",), wr=(f"outd_{si}_{t}",))
            sch.fence((), tuple(f"pt{k}" for k in range(NT)) +
                      tuple(f"cx{d}" for d in range(8)) +
                      fm_keys("h0") + fm_keys("g") + fm_keys("r") +
                      qk_keys("q") + qk_keys("k") +
                      tuple(f"v_{t}_{o}" for t in range(NT) for o in range(2)) +
                      ("w1_0", "w1_1") + tuple(f"w1res{d}" for d in range(8)))
            return None

        sch.fence(tuple(f"pt{k}" for k in range(NT)) +
                  tuple(f"cx{d}" for d in range(8)), fm_keys("yf"))
        sch.fence(qk_keys("q"), tuple(f"moe_{t}" for t in range(NT)))
        sch.fence(tuple(f"v_{t}_{o}" for t in range(NT) for o in range(2)),
                  ("esl0", "esl1"))
        for t in range(NT):
            st = t % 2
            yslot = y1s[:, st * 1024:st * 1024 + 1024]
            yk = f"y1s{st}"
            DMA("sync", lambda e, tt=t, y=yslot: e.dma_start(
                out=y, in_=d_y1[tt * 128:(tt + 1) * 128, :]),
                rd=(f"y1d_{t}",), wr=(yk,))
            tm_to_fm(yslot, arX, t, (yk,), "yf")
            for dc in range(8):
                T(mm(psL[0:8, 0:128], gwt[:, dc * 8:(dc + 1) * 8],
                     arX[:, dc * S + t * 128: dc * S + t * 128 + 128],
                     dc == 0, dc == 7),
                  rd=fm_keys("yf", dc, [t]) + ("gwt",), wr=("psL",))
            A_(lambda e: e.activation(scr[0:8, 720:848], psL[0:8, 0:128],
                                      AF.Identity, bias=gbb[0:8, 0:1]),
               rd=("psL", "gbb"), wr=("glog",))
            T(lambda e: e.transpose(psL[:, 0:8], scr[0:8, 720:848], identf[0:8, 0:8]),
              rd=("glog", "identf"), wr=("psL",))
            gcols = gts[:, t * 16: t * 16 + 8]
            wcols = gts[:, t * 16 + 8: t * 16 + 16]
            gk2, wk2 = f"g8_{t}", f"w8_{t}"
            V(lambda e, g=gcols: e.tensor_copy(g, psL[:, 0:8]),
              rd=("psL",), wr=(gk2,))
            V(lambda e, g=gcols: e.reduce_max(sc2[:, 96:97], g, axis=AX),
              rd=(gk2,), wr=("gmax",))
            V(lambda e: e.tensor_scalar(sc2[:, 97:98], sc2[:, 96:97], -1.0, None,
                                        OP.mult), rd=("gmax",), wr=("gnm",))
            A_(lambda e, g=gcols: e.activation(g, g, AF.Exp, bias=sc2[:, 97:98]),
               rd=(gk2, "gnm"), wr=(gk2,))
            V(lambda e, g=gcols: e.reduce_sum(sc2[:, 98:99], g, axis=AX),
              rd=(gk2,), wr=("gsum",))
            V(lambda e: e.reciprocal(sc2[:, 99:100], sc2[:, 98:99]),
              rd=("gsum",), wr=("grcp",))
            V(lambda e, g=gcols: e.tensor_scalar(g, g, sc2[:, 99:100], None, OP.mult),
              rd=(gk2, "grcp"), wr=(gk2,))
            V(lambda e, g=gcols: e.reduce_max(sc2[:, 100:101], g, axis=AX),
              rd=(gk2,), wr=("gm1",))
            V(lambda e, g=gcols: e.tensor_scalar(scr[:, 704:712], g, sc2[:, 100:101],
                                                 None, OP.is_ge),
              rd=(gk2, "gm1"), wr=("gmask",))
            V(lambda e, g=gcols: e.scalar_tensor_tensor(
                scr[:, 712:720], scr[:, 704:712], -1e30, g, OP.mult, OP.add),
              rd=("gmask", gk2), wr=("gmasked",))
            V(lambda e: e.reduce_max(sc2[:, 101:102], scr[:, 712:720], axis=AX),
              rd=("gmasked",), wr=("gm2",))
            V(lambda e, g=gcols: e.tensor_scalar(scr[:, 704:712], g, sc2[:, 101:102],
                                                 None, OP.is_ge),
              rd=(gk2, "gm2"), wr=("gmask",))
            V(lambda e, g=gcols, w=wcols: e.tensor_mul(w, g, scr[:, 704:712]),
              rd=(gk2, "gmask"), wr=(wk2,))
            V(lambda e, w=wcols: e.tensor_copy(wg8[:, 0:8], w),
              rd=(wk2,), wr=("wg8",))
            T(lambda e: e.transpose(ptr[0:8, 0:128], wg8[:, 0:8], identb[:, :]),
              rd=("wg8", "identb"), wr=("ptr0",))
            V(lambda e: e.tensor_copy(mrow[0:8, 0:128], ptr[0:8, 0:128]),
              rd=("ptr0",), wr=("wg8T",))
            for vc in range(2):
                T(mm(psS[1][:, vc * 512:vc * 512 + 512], mrow[0:8, 0:128],
                     geb[0:8, vc * 512:vc * 512 + 512], True, True),
                  rd=("wg8T", "geb"), wr=(f"psS1_{vc}",))
            V(lambda e, tt=t: e.tensor_copy(
                arQ[:, tt * DM:(tt + 1) * DM], psS[1][:, :]),
              rd=("psS1_0", "psS1_1"), wr=(f"moe_{t}",))
        for ei in range(NE):
            eoff = (ei % 2) * 8192
            ek = f"esl{ei % 2}"
            DMA("gpsimd", lambda e, ee=ei, off=eoff: e.dma_start(
                out=arV[:, off:off + 8192], in_=d_wexp[ee]),
                rd=(), wr=(ek,))
            for t in range(NT):
                un = t % 2
                for dc in range(8):
                    for fh in range(2):
                        T(mm(psS[un][:, fh * 512:fh * 512 + 512],
                             arX[:, dc * S + t * 128: dc * S + t * 128 + 128],
                             arV[:, eoff + dc * DM + fh * 512:
                                 eoff + dc * DM + fh * 512 + 512],
                             dc == 0, dc == 7),
                          rd=(ek,) + fm_keys("yf", dc, [t]),
                          wr=(f"psS{un}_{fh}",))
                for fh in range(2):
                    A_(lambda e, u=un, f=fh, tt=t, ee=ei: e.activation(
                        etmp[:, f * 512:f * 512 + 512], psS[u][:, f * 512:f * 512 + 512],
                        AF.Identity, scale=gts[:, tt * 16 + 8 + ee:tt * 16 + 9 + ee]),
                       rd=(f"psS{un}_{fh}", f"w8_{t}"), wr=(f"etmp{fh}",))
                    V(lambda e, f=fh, tt=t: e.tensor_add(
                        arQ[:, tt * DM + f * 512: tt * DM + f * 512 + 512],
                        arQ[:, tt * DM + f * 512: tt * DM + f * 512 + 512],
                        etmp[:, f * 512:f * 512 + 512]),
                      rd=(f"etmp{fh}", f"moe_{t}"), wr=(f"moe_{t}",))

        for t in range(NT):
            DMA("sync", lambda e, tt=t: e.dma_start(
                out=d_moe[tt * 128:(tt + 1) * 128, :],
                in_=arQ[:, tt * DM:(tt + 1) * DM]),
                rd=(f"moe_{t}",), wr=(f"moed_{t}",))
        sch.fence(tuple(f"moe_{t}" for t in range(NT)), qk_keys("q"))

        def finals_tile(t):
            st = 2 + t % 2
            yslot = y1s[:, (t % 2) * 1024:(t % 2) * 1024 + 1024]
            yk = f"y1s{t % 2}"
            DMA("sync", lambda e, tt=t, y=yslot: e.dma_start(
                out=y, in_=d_y1[tt * 128:(tt + 1) * 128, :]),
                rd=(f"y1d_{t}",), wr=(yk,))
            DMA("sync", lambda e, tt=t: e.dma_start(
                out=mslot[:, :], in_=d_moe[tt * 128:(tt + 1) * 128, :]),
                rd=(f"moed_{t}",), wr=("mslot",))
            fslot = fsb[:, (t % 2) * 1024:(t % 2) * 1024 + 1024]
            fk = f"fs{t % 2}"
            V(lambda e, f=fslot, y=yslot: e.tensor_add(f, y, mslot[:, :]),
              rd=(yk, "mslot"), wr=(fk,))
            ln_stats_v(fslot, (fk,), st)
            ln_apply(fslot, fslot, (fk,), (fk,), st)
            V(lambda e, f=fslot: e.tensor_mul(f, f, reps[:, 4 * DM:5 * DM]),
              rd=(fk, "reps4"), wr=(fk,))
            V(lambda e, f=fslot: e.tensor_add(f, f, reps[:, 5 * DM:6 * DM]),
              rd=(fk, "reps5"), wr=(fk,))
            ln_stats_v(fslot, (fk,), st)
            ln_apply(fslot, fslot, (fk,), (fk,), st)
            xslot = xs[:, (t % 2) * 1024:(t % 2) * 1024 + 1024]
            xk = f"xs{t % 2}"
            DMA("sync", lambda e, tt=t, d=xslot: e.dma_start(
                out=d, in_=d_u[si, tt * 128:(tt + 1) * 128, :]),
                rd=(f"u_{t}",), wr=(xk,))
            V(lambda e, f=fslot, x_=xslot: e.tensor_add(f, f, x_),
              rd=(fk, xk), wr=(fk,))
            DMA("sync", lambda e, tt=t, f=fslot: e.dma_start(
                out=d_out[si, tt * 128:(tt + 1) * 128, :], in_=f),
                rd=(fk,), wr=(f"outd_{si}_{t}",))

        sch.fence(fm_keys("yf") + ("esl0", "esl1") +
                  tuple(f"w1res{d}" for d in range(8)),
                  fm_keys("h0") + qk_keys("k") +
                  tuple(f"v_{t}_{o}" for t in range(NT) for o in range(2)) +
                  ("w1_0", "w1_1"))
        return (finals_tile,)

    tail = None
    for si in range(NSEQ):
        tail = emit_seq(si, tail_cb=tail)
    if tail is not None and upto == "full":
        for t in range(NT):
            tail[0](t)

    allout = [k for k in sch.last_w if k.startswith("outd_")]
    sch.add("sync", None, rd=tuple(allout), wr=())

    n = sch.finalize(nc, es)
    es.close()
    return nc, n


_BUILD_CACHE = {}


def _stage(inputs):
    f32 = np.float32
    bf = ml_dtypes.bfloat16

    def c(a):
        return np.ascontiguousarray(a, dtype=f32)

    st = {}
    st["wqkv"] = c(np.asarray(inputs["L0_attn_W_qkv"], f32)
                   .reshape(8, 128, 6, 512).transpose(2, 0, 1, 3))
    b_qkv = np.asarray(inputs["L0_attn_b_qkv"], f32)
    st["bqkv_fm"] = c(b_qkv.reshape(24, 128).T)
    st["bqkv_row"] = c(b_qkv.reshape(1, 3072))
    st["wout"] = c(np.asarray(inputs["L0_attn_W_out"], f32).reshape(8, 128, DM))
    st["bout_rep"] = c(np.broadcast_to(np.asarray(inputs["L0_attn_b_out"], f32),
                                       (128, DM)))
    ip = np.asarray(inputs["L0_ffn_in_proj_w"], f32)
    st["winproj"] = c(np.stack([
        ip[i * DM:(i + 1) * DM].T.reshape(8, 128, 2, 512).transpose(2, 0, 1, 3)
        for i in range(3)]))
    bip = np.asarray(inputs["L0_ffn_in_proj_b"], f32)
    st["binproj_fm"] = c(bip.reshape(24, 128).T)
    st["binproj_row"] = c(bip.reshape(1, 3072))
    st["woutproj"] = c(np.asarray(inputs["L0_ffn_out_proj_w"], f32).T
                       .reshape(8, 128, DM))
    st["boutproj_rep"] = c(np.broadcast_to(
        np.asarray(inputs["L0_ffn_out_proj_b"], f32), (128, DM)))
    gw = np.asarray(inputs["L0_ffn_gate_w"], f32).T
    st["wgate"] = c(gw.reshape(8, 128, 8).transpose(1, 0, 2).reshape(128, 64))
    st["bgate"] = c(np.asarray(inputs["L0_ffn_gate_b"], f32).reshape(8, 1))
    we = np.asarray(inputs["L0_ffn_expert_w"], f32)
    st["wexp"] = c(np.stack([
        we[e].T.reshape(8, 128, DM).transpose(1, 0, 2).reshape(128, 8 * DM)
        for e in range(NE)]))
    st["bexp"] = c(np.asarray(inputs["L0_ffn_expert_b"], f32))
    st["ln1w_rep"] = c(np.broadcast_to(np.asarray(inputs["L0_ffn_ln1_w"], f32),
                                       (128, DM)))
    st["ln1b_rep"] = c(np.broadcast_to(np.asarray(inputs["L0_ffn_ln1_b"], f32),
                                       (128, DM)))
    st["ln2w_rep"] = c(np.broadcast_to(np.asarray(inputs["L0_ffn_ln2_w"], f32),
                                       (128, DM)))
    st["ln2b_rep"] = c(np.broadcast_to(np.asarray(inputs["L0_ffn_ln2_b"], f32),
                                       (128, DM)))
    st["identb"] = np.eye(128, dtype=bf)
    st["identf"] = np.eye(128, dtype=f32)
    st["onescol"] = np.ones((128, 1), dtype=bf)
    rowc = np.zeros((1, 384), dtype=bf)
    rowc[0, 0:128] = 1.0
    rowc[0, 128:192] = 1.0
    rowc[0, 320:384] = 1.0
    st["rowconst"] = rowc
    st["trimask"] = np.triu(np.ones((128, 128), dtype=f32)).astype(bf)
    inv = (1.0 / (10000.0 ** (np.arange(0, HD, 2, dtype=f32) / f32(HD)))).astype(f32)
    fr = (np.arange(S, dtype=f32)[:, None] * inv[None, :]).astype(f32)
    st["ce"] = c(np.tile(np.cos(fr).astype(f32), (1, NH)))
    st["se"] = c(np.tile(np.sin(fr).astype(f32), (1, NH)))
    return st


def kernel(**inputs):
    if "full" not in _BUILD_CACHE:
        _BUILD_CACHE["full"] = build_nc("full")
    nc, _ = _BUILD_CACHE["full"]
    st = _stage(inputs)
    x = np.ascontiguousarray(np.asarray(inputs["L0_attn_x"], np.float32))
    in_maps = []
    for ci in range(NCORES):
        m = dict(st)
        m["x"] = x[ci * NSEQ:(ci + 1) * NSEQ]
        in_maps.append(m)
    res = run_bass_kernel_spmd(nc, in_maps, core_ids=list(range(NCORES)))
    out = np.concatenate([np.asarray(r["out"]) for r in res.results], axis=0)
    return out.reshape(16, S, DM).astype(np.float32)
